# version 09 (final)
# speedup vs baseline: 1.0591x; 1.0091x over previous
import math
from contextlib import ExitStack

import numpy as np
import ml_dtypes

import concourse.bass as bass
import concourse.mybir as mybir
from concourse.bass_utils import run_bass_kernel_spmd

F32 = mybir.dt.float32
BF16 = mybir.dt.bfloat16
I32 = mybir.dt.int32
AF = mybir.ActivationFunctionType
ALU = mybir.AluOpType
AXL = mybir.AxisListType
BF = ml_dtypes.bfloat16

NCORE = 8
TC = 1024
D = 2048
DCH = D // 128
EPS_LN = 1e-5


_ENG_STREAMS = ("T", "V", "A", "P", "S")


def _issuer(stream):
    if stream.startswith("d"):
        return {"dS": "S", "dP": "P", "dA": "A"}[stream[:2]]
    return stream


class Prog:
    def __init__(self, nc):
        self.nc = nc
        self.ops = []
        from collections import defaultdict
        self.n_in_stream = defaultdict(int)

    def emit(self, stream, fn, deps=()):
        idx = self.n_in_stream[stream]
        self.n_in_stream[stream] += 1
        amount = 16 if stream.startswith("d") else 1
        self.ops.append((stream, fn, tuple(d for d in deps if d is not None), amount, idx))
        return (stream, idx)

    def build(self, es):
        nc = self.nc
        streams = sorted(self.n_in_stream.keys())
        sems = {s: es.enter_context(nc.semaphore(f"sem_{s.replace(':', '_')}"))
                for s in streams}
        marked = set()
        for stream, _, deps, _, idx in self.ops:
            if stream.startswith("d"):
                marked.add((stream, idx))
            for d in deps:
                marked.add(d)
        run = {s: 0 for s in streams}
        pos = {}
        for stream, _, _, amount, idx in self.ops:
            if (stream, idx) in marked:
                run[stream] += amount
            pos[(stream, idx)] = run[stream]

        eng_ops = {"T": [], "V": [], "A": [], "P": [], "S": []}
        for i, (stream, fn, deps, amount, idx) in enumerate(self.ops):
            eng_ops[_issuer(stream)].append((stream, fn, deps, amount, idx, (stream, idx) in marked))

        block = es.enter_context(nc.Block())

        def section(eng_name, attr):
            ops = eng_ops[eng_name]
            if not ops:
                return

            def body(eng):
                hw = {s: 0 for s in streams}
                for stream, fn, deps, amount, idx, is_marked in ops:
                    for d in deps:
                        tgt = pos[d]
                        ds = d[0]
                        if tgt > hw[ds]:
                            eng.wait_ge(sems[ds], tgt)
                            hw[ds] = tgt
                    ins = fn(eng)
                    if is_marked:
                        assert ins is not None, f"marked op on {stream} returned no instruction"
                        ins.then_inc(sems[stream], amount)

            getattr(block, attr)(body)

        section("S", "sync")
        section("P", "gpsimd")
        section("T", "tensor")
        section("V", "vector")
        section("A", "scalar")
        return sems

    def finish(self, out_dma_tokens):
        tail = {}
        for stream, _, _, _, idx in self.ops:
            if stream.startswith("d"):
                tail[stream] = (stream, idx)
        deps = tuple(out_dma_tokens) + tuple(tail.values())
        self.emit("S", lambda eng: eng.nop(), deps=deps)


def ap_rep(ap, reps):
    return bass.AP(ap.tensor, ap.offset, [list(q) for q in ap.ap] + [[0, reps]])


def ap_raw(t, offset, dims):
    return bass.AP(t.tensor if isinstance(t, bass.AP) else t, offset, dims)



def build_ffn(F, R, sel_val, out_scale):
    ER = 8 * R
    NEC = ER // 128
    NF = F // 128
    nc = bass.Bass("TRN2", target_bir_lowering=False)
    dt_in = {}

    def din(name, shape, dt=F32):
        dt_in[name] = (shape, dt)
        return nc.dram_tensor(name, shape, dt, kind="ExternalInput").ap()

    x = din("x", [DCH, 128, TC])
    wgT = din("wgT", [DCH, 128, 8])
    w1R = din("w1R", [NF, DCH, 128, 128])
    w3R = din("w3R", [NF, DCH, 128, 128])
    w2M = din("w2M", [DCH, NF, 128, 128])
    a1T = din("a1T", [DCH, 128, ER])
    a3T = din("a3T", [DCH, 128, ER])
    a2R = din("a2R", [NF, 128, ER])
    b1R = din("b1R", [NF, NEC, 128, 128])
    b3R = din("b3R", [NF, NEC, 128, 128])
    b2M = din("b2M", [16, NEC, 128, 128])
    ssel_d = din("ssel", [8, ER])
    onesc_d = din("onesc", [128, 1])
    onesf_d = din("onesf", [1, 128])
    identf_d = din("identf", [128, 128])
    xo = nc.dram_tensor("xo", [DCH, 128, TC], F32, kind="ExternalOutput").ap()

    p = Prog(nc)
    es = ExitStack()
    with es:
        sb = lambda n, shape, dt: es.enter_context(nc.sbuf_tensor(n, shape, dt))
        u = sb("u", [128, DCH * TC], BF16)
        acc = sb("acc", [128, (NF + 1) * TC], BF16)
        warena = sb("warena", [128, 4 * 2048], BF16)
        sb1 = sb("sb1", [128, 2 * NEC * 128], BF16)
        sb3 = sb("sb3", [128, 2 * NEC * 128], BF16)
        sa2b = sb("sa2b", [128, 2 * ER], BF16)
        sb2s = sb("sb2s", [128, 2 * NEC * 128], BF16)
        sa1 = sb2s

        swg = sb("swg", [128, DCH * 8], F32)
        ma10 = sb("ma10", [128, NEC * TC], BF16)
        da1 = sb("da1", [128, NEC * TC], BF16)
        ma30 = sb("ma30", [128, NEC * TC], BF16)
        da3 = sb("da3", [128, NEC * TC], BF16)
        rw0b = sb("rw0b", [128, TC], BF16)
        rw1b = sb("rw1b", [128, TC], BF16)
        xst = sb("xst", [128, TC], F32)
        rows2 = sb("rows2", [1, 2 * TC + 16], F32)
        nmean = rows2[0:1, 0:TC]
        rstdr = rows2[0:1, TC:2 * TC]
        wg1r = rows2[0:1, 2 * TC:2 * TC + 8]
        lgsb = sb("lgsb", [72, TC], F32)
        rtA = sb("rtA", [128, 8 * 9], F32)
        rtB = sb("rtB", [128, 8 * 72], F32)
        mcol = sb("mcol", [128, 4], F32)
        scr8 = sb("scr8", [128, 8], F32)
        onesc = sb("s_onesc", [128, 1], F32)
        onesc_b = sb("onesc_b", [128, 1], BF16)
        onesf = sb("s_onesf", [1, 128], F32)
        identf = sb("s_identf", [128, 128], F32)
        ssel = sb("s_ssel", [72, ER], F32)

        sa3 = warena
        pA = es.enter_context(nc.psum_tensor("pA", [128, TC], F32))
        pB = es.enter_context(nc.psum_tensor("pB", [128, TC], F32))
        pE = es.enter_context(nc.psum_tensor("pE", [128, NEC * TC], F32))

        E = p.emit
        H = 512

        a2p0 = lambda c: u[:, c * TC:(c + 1) * TC]
        mbigp = lambda m, c: u[:, (NEC + m * NEC + c) * TC:(NEC + m * NEC + c + 1) * TC]
        mbig_pre = lambda m, c: acc[:, (NF - 2 * NEC + m * NEC + c) * TC:(NF - 2 * NEC + m * NEC + c + 1) * TC]

        tc0 = E("dA:c", lambda e: e.dma_start(out=onesc[:], in_=onesc_d))
        tc1 = E("dP:c", lambda e: e.dma_start(out=onesc_b[:], in_=onesc_d))
        tc2 = E("dA:c", lambda e: e.dma_start(out=onesf[:], in_=onesf_d), deps=(tc0,))
        tc3 = E("dA:c", lambda e: e.dma_start(out=identf[:], in_=identf_d), deps=(tc2,))
        tc4 = E("dA:c", lambda e: e.dma_start(out=ssel[32:40, :], in_=ssel_d), deps=(tc3,))
        tc4b = E("dA:c", lambda e: e.dma_start(out=ssel[64:72, :], in_=ssel_d), deps=(tc4,))
        tc5 = E("dA:c", lambda e: e.dma_start(out=swg[:].rearrange("p (c e) -> p c e", c=DCH), in_=wgT.rearrange("c p e -> p c e")), deps=(tc4b,))
        CONSTS = (tc0, tc1, tc2, tc3, tc4, tc4b, tc5)

        xld = {}
        xst_users = {}
        sum_mm = []
        sq_mm = []
        wg1_mm = []
        prev_u_write = {}
        for i in range(DCH):
            for h in range(2):
                bslot = (2 * i + h) % 2
                deps = xst_users.pop(bslot, ())
                xld[(0, i, h)] = E(f"dS:x{bslot}", lambda e, i=i, h=h, b=bslot: e.dma_start(
                    out=xst[:, b * H:(b + 1) * H], in_=x[i, :, h * H:(h + 1) * H]), deps=deps)
                m1 = E("T", lambda e, i=i, h=h, b=bslot: e.matmul(
                    pE[0:1, h * H:(h + 1) * H], onesc[:], xst[:, b * H:(b + 1) * H],
                    start=(i == 0), stop=(i == DCH - 1)), deps=(xld[(0, i, h)], tc0))
                sum_mm.append(m1)
                sq = E("A", lambda e, i=i, h=h, b=bslot: e.activation(
                    out=u[:, i * TC + h * H: i * TC + (h + 1) * H], in_=xst[:, b * H:(b + 1) * H],
                    func=AF.Square), deps=(xld[(0, i, h)],))
                m2 = E("T", lambda e, i=i, h=h: e.matmul(
                    pE[32:33, h * H:(h + 1) * H], onesc_b[:], u[:, i * TC + h * H: i * TC + (h + 1) * H],
                    start=(i == 0), stop=(i == DCH - 1)), deps=(sq, tc1))
                sq_mm.append(m2)
                xst_users[bslot] = (m1, sq)
                prev_u_write[(i, h)] = m2
            wg1_mm.append(E("T", lambda e, i=i: e.matmul(
                pA[0:1, 0:8], onesc[:], swg[:, i * 8:(i + 1) * 8],
                start=(i == 0), stop=(i == DCH - 1)), deps=(tc5, tc0)))

        t_nm = E("V", lambda e: e.tensor_scalar(out=nmean, in0=pE[0:1, 0:TC], scalar1=-1.0 / D,
                                                scalar2=None, op0=ALU.mult), deps=(sum_mm[-1],))
        rowB = xst[0:1, 0:TC]
        t_msq = E("V", lambda e: e.tensor_scalar(out=rowB, in0=pE[32:33, 0:TC], scalar1=1.0 / D,
                                                 scalar2=None, op0=ALU.mult),
                  deps=(sq_mm[-1], xst_users[0][0], xst_users[0][1], xst_users[1][0], xst_users[1][1]))
        t_e2 = E("V", lambda e: e.tensor_tensor(out=rstdr, in0=nmean, in1=nmean, op=ALU.mult),
                 deps=(t_nm,))
        t_var = E("V", lambda e: e.tensor_tensor(out=rowB, in0=rowB, in1=rstdr, op=ALU.subtract),
                  deps=(t_msq, t_e2))
        t_vE = E("V", lambda e: e.tensor_scalar(out=rowB, in0=rowB, scalar1=EPS_LN, scalar2=None,
                                                op0=ALU.add), deps=(t_var,))
        t_ln = E("A", lambda e: e.activation(out=rowB, in_=rowB, func=AF.Ln), deps=(t_vE,))
        t_rstd = E("A", lambda e: e.activation(out=rstdr, in_=rowB, func=AF.Exp, scale=-0.5),
                   deps=(t_ln, t_e2))
        t_wg1 = E("V", lambda e: e.tensor_copy(out=wg1r, in_=pA[0:1, 0:8]), deps=(wg1_mm[-1],))

        bc_mm = []
        for h in range(2):
            bc_mm.append(E("T", lambda e, h=h: e.matmul(
                pA[:, h * H:(h + 1) * H], onesf[:], nmean[0:1, h * H:(h + 1) * H],
                start=True, stop=True), deps=(t_nm, t_wg1, tc2)))
        for h in range(2):
            bc_mm.append(E("T", lambda e, h=h: e.matmul(
                pB[:, h * H:(h + 1) * H], onesf[:], rstdr[0:1, h * H:(h + 1) * H],
                start=True, stop=True), deps=(t_rstd,)))

        lg_mm = []
        u_done = {}
        for i in range(DCH):
            for h in range(2):
                bslot = (2 * i + h) % 2
                deps = xst_users.pop(bslot, ())
                if i == 0:
                    deps = deps + (t_msq, t_var, t_vE, t_ln, t_rstd)
                ld = E(f"dS:x{bslot}", lambda e, i=i, h=h, b=bslot: e.dma_start(
                    out=xst[:, b * H:(b + 1) * H], in_=x[i, :, h * H:(h + 1) * H]), deps=deps)
                xld[(1, i, h)] = ld
                mlg = E("T", lambda e, i=i, h=h, b=bslot: e.matmul(
                    pE[0:8, h * H:(h + 1) * H], swg[:, i * 8:(i + 1) * 8], xst[:, b * H:(b + 1) * H],
                    start=(i == 0), stop=False), deps=(ld,))
                lg_mm.append(mlg)
                t1 = E("V", lambda e, b=bslot, h=h: e.tensor_tensor(
                    out=xst[:, b * H:(b + 1) * H], in0=xst[:, b * H:(b + 1) * H],
                    in1=pA[:, h * H:(h + 1) * H], op=ALU.add), deps=(ld, mlg, bc_mm[h]))
                t2 = E("V", lambda e, i=i, h=h, b=bslot: e.tensor_tensor(
                    out=u[:, i * TC + h * H: i * TC + (h + 1) * H], in0=xst[:, b * H:(b + 1) * H],
                    in1=pB[:, h * H:(h + 1) * H], op=ALU.mult),
                    deps=(t1, bc_mm[2 + h], prev_u_write[(i, h)]))
                u_done[(i, h)] = t2
                xst_users[bslot] = (t2,)
        for h in range(2):
            lg_mm.append(E("T", lambda e, h=h: e.matmul(
                pE[0:8, h * H:(h + 1) * H], wg1r, nmean[0:1, h * H:(h + 1) * H],
                start=False, stop=True), deps=(t_wg1, lg_mm[-1])))

        c_lg = E("V", lambda e: e.tensor_copy(out=lgsb[0:8, :], in_=pE[0:8, 0:TC]), deps=(lg_mm[-1],))
        c_rs = c_lg
        tr_in = []
        for j in range(8):
            tr_in.append(E("T", lambda e, j=j: e.transpose(
                pA[:, j * 16:j * 16 + 8], lgsb[0:8, j * 128:(j + 1) * 128], identf[0:8, 0:8]),
                deps=(c_lg, tc3, u_done[(DCH - 1, 1)], lg_mm[-1]) + tuple(tr_in[-1:])))
            tr_in.append(E("T", lambda e, j=j: e.transpose(
                pA[:, j * 16 + 8:j * 16 + 9], rstdr[0:1, j * 128:(j + 1) * 128], identf[0:1, 0:1]),
                deps=(t_rstd, tc3)))
        rt_ops = []
        cA_prev = []
        for j in range(8):
            cA = E("V", lambda e, j=j: e.tensor_copy(out=rtA[:, j * 9:(j + 1) * 9],
                                                     in_=pA[:, j * 16:j * 16 + 9]), deps=(tr_in[2 * j], tr_in[2 * j + 1]))
            cA_prev.append(cA)
            o1 = E("V", lambda e, j=j: e.reduce_max(out=mcol[:, 0:1], in_=rtA[:, j * 9:j * 9 + 8],
                                                    axis=AXL.X), deps=(cA,))
            o2 = E("V", lambda e, j=j: e.tensor_scalar(out=rtB[:, j * 72 + 32:j * 72 + 40],
                                                       in0=rtA[:, j * 9:j * 9 + 8], scalar1=mcol[:, 0:1],
                                                       scalar2=None, op0=ALU.is_ge), deps=(o1,))
            o3 = E("V", lambda e, j=j: e.scalar_tensor_tensor(out=scr8[:], in0=rtB[:, j * 72 + 32:j * 72 + 40],
                                                              scalar=-1e30, in1=rtA[:, j * 9:j * 9 + 8],
                                                              op0=ALU.mult, op1=ALU.add), deps=(o2,))
            o4 = E("V", lambda e: e.reduce_max(out=mcol[:, 1:2], in_=scr8[:], axis=AXL.X), deps=(o3,))
            o5 = E("V", lambda e, j=j: e.tensor_scalar(out=rtB[:, j * 72 + 64:j * 72 + 72], in0=scr8[:],
                                                       scalar1=mcol[:, 1:2], scalar2=None,
                                                       op0=ALU.is_ge), deps=(o4,))
            o6 = E("V", lambda e: e.tensor_tensor(out=mcol[:, 2:3], in0=mcol[:, 0:1], in1=mcol[:, 1:2],
                                                  op=ALU.subtract),
                   deps=(o4, o5) + ((rt_ops[-1],) if rt_ops else ()))
            o7 = E("A", lambda e, j=j: e.activation(out=rtB[:, j * 72:j * 72 + 1],
                                                    in_=mcol[:, 2:3], func=AF.Sigmoid,
                                                    scale=rtA[:, j * 9 + 8:j * 9 + 9]), deps=(o6, cA))
            rt_ops.append(o7)
        tr_out = []
        for j in range(8):
            tr_out.append(E("T", lambda e, j=j: e.transpose(
                pB[0:72, j * 128:(j + 1) * 128], rtB[:, j * 72:j * 72 + 72], identf[:]),
                deps=(rt_ops[j], bc_mm[2], bc_mm[3]) + tuple(u_done[(i, h)] for (i, h) in [(DCH - 1, 0), (DCH - 1, 1)])))
        c_f0 = E("V", lambda e: e.tensor_copy(out=lgsb[0:1, :], in_=pB[0:1, :]),
                 deps=tuple(tr_out) + (c_lg, c_rs) + tuple(tr_in))
        c_f1 = E("V", lambda e: e.tensor_copy(out=lgsb[32:40, :], in_=pB[32:40, :]),
                 deps=tuple(tr_out))
        c_fm = E("V", lambda e: e.tensor_copy(out=lgsb[64:72, :], in_=pB[64:72, :]),
                 deps=tuple(tr_out) + (c_f0, c_f1))
        rwmm = []
        for h in range(2):
            rwmm.append(E("T", lambda e, h=h: e.matmul(
                pA[:, h * H:(h + 1) * H], onesf[:], lgsb[0:1, h * H:(h + 1) * H],
                start=True, stop=True), deps=(c_fm,) + tuple(tr_in)))
        rw_ev = []
        rw_ev.append(E("V", lambda e: e.tensor_copy(out=rw0b[:], in_=pA[:]), deps=tuple(rwmm)))
        rw_ev.append(E("V", lambda e: e.tensor_scalar(out=rw1b[:], in0=rw0b[:], scalar1=-1.0,
                                                      scalar2=1.0, op0=ALU.mult, op1=ALU.add),
                      deps=(rw_ev[0],)))
        mb_ev = {}
        for m in range(2):
            mm = []
            for c in range(NEC):
                for h in range(2):
                    mm.append(E("T", lambda e, m=m, c=c, h=h: e.matmul(
                        pE[:, c * TC + h * H: c * TC + (h + 1) * H], ssel[32 * (m + 1):32 * (m + 1) + 8, c * 128:(c + 1) * 128],
                        lgsb[32 * (m + 1):32 * (m + 1) + 8, h * H:(h + 1) * H], start=True, stop=True),
                        deps=(c_fm, tc4, tc4b) + ((mb_ev[(0, c)],) if m == 1 else (lg_mm[-1], c_lg))))
            for c in range(NEC):
                mb_ev[(m, c)] = E("V", lambda e, m=m, c=c: e.tensor_copy(
                    out=mbig_pre(m, c), in_=pE[:, c * TC:(c + 1) * TC]), deps=tuple(mm))

        def a_masked(aT, sabuf, ma0, dal, extra_deps):
            mms = []
            last_by_slot = {}
            for kc in range(DCH):
                slot = kc % 2
                dep_w = last_by_slot.pop(slot, None)
                ld = E(f"dP:a{slot}", lambda e, kc=kc, s=slot: e.dma_start(
                    out=sabuf[:, s * ER:(s + 1) * ER], in_=aT[kc]),
                    deps=(dep_w,) if dep_w else extra_deps)
                kc_mm = []
                for c in range(NEC):
                    for h in range(2):
                        kc_mm.append(E("T", lambda e, kc=kc, s=slot, c=c, h=h: e.matmul(
                            pE[:, c * TC + h * H: c * TC + (h + 1) * H],
                            sabuf[:, s * ER + c * 128: s * ER + (c + 1) * 128],
                            u[:, kc * TC + h * H: kc * TC + (h + 1) * H],
                            start=(kc == 0), stop=(kc == DCH - 1)),
                            deps=(ld, u_done[(kc, h)]) + tuple(extra_deps)))
                mms.extend(kc_mm)
                last_by_slot[slot] = kc_mm[-1]
            outs = []
            for c in range(NEC):
                t0 = E("V", lambda e, c=c: e.tensor_tensor(
                    out=ma0[:, c * TC:(c + 1) * TC], in0=pE[:, c * TC:(c + 1) * TC],
                    in1=mbig_pre(0, c), op=ALU.mult), deps=(mms[-1], mb_ev[(0, c)]))
                t1 = E("V", lambda e, c=c: e.tensor_tensor(
                    out=dal[:, c * TC:(c + 1) * TC], in0=pE[:, c * TC:(c + 1) * TC],
                    in1=mbig_pre(1, c), op=ALU.mult), deps=(mms[-1], mb_ev[(1, c)]))
                t2 = E("V", lambda e, c=c: e.tensor_tensor(
                    out=dal[:, c * TC:(c + 1) * TC], in0=dal[:, c * TC:(c + 1) * TC],
                    in1=ma0[:, c * TC:(c + 1) * TC], op=ALU.subtract), deps=(t0, t1))
                outs.append(t2)
            return outs

        a1_done = a_masked(a1T, sa1, ma10, da1, (mb_ev[(1, NEC - 1)],))
        a3_done = a_masked(a3T, sa3, ma30, da3, tuple(a1_done))

        W1OFF, W3OFF = 0, 2 * 2048
        w_last_use = {}
        a2_mm_last = None
        prev_accslot = {}
        prev_pA_consumed = tuple(a3_done) + tuple(rw_ev)
        prev_pB_consumed = tuple(a3_done)
        prev_pE_guard = tuple(a3_done)
        acc_done = []
        for fc in range(NF):
            s = fc % 2
            wdep = w_last_use.pop(s, (a3_done[-1],))
            ldw1 = E(f"dP:w1{s}", lambda e, fc=fc, s=s: e.dma_start(
                out=warena[:, W1OFF + s * 2048: W1OFF + (s + 1) * 2048].rearrange("p (c f) -> p c f", c=DCH),
                in_=w1R[fc].rearrange("c p f -> p c f")), deps=wdep)
            ldw3 = E(f"dP:w3{s}", lambda e, fc=fc, s=s: e.dma_start(
                out=warena[:, W3OFF + s * 2048: W3OFF + (s + 1) * 2048].rearrange("p (c f) -> p c f", c=DCH),
                in_=w3R[fc].rearrange("c p f -> p c f")), deps=wdep)
            ldb1 = E(f"dP:b1{s}", lambda e, fc=fc, s=s: e.dma_start(
                out=sb1[:, s * NEC * 128:(s + 1) * NEC * 128].rearrange("p (c f) -> p c f", c=NEC),
                in_=b1R[fc].rearrange("c p f -> p c f")), deps=wdep)
            ldb3 = E(f"dP:b3{s}", lambda e, fc=fc, s=s: e.dma_start(
                out=sb3[:, s * NEC * 128:(s + 1) * NEC * 128].rearrange("p (c f) -> p c f", c=NEC),
                in_=b3R[fc].rearrange("c p f -> p c f")), deps=wdep)
            lda2 = E(f"dP:a2{s}", lambda e, fc=fc, s=s: e.dma_start(
                out=sa2b[:, s * ER:(s + 1) * ER], in_=a2R[fc]), deps=wdep)

            mmsA, mmsB = [], []
            for kc in range(DCH):
                for h in range(2):
                    mmsA.append(E("T", lambda e, kc=kc, h=h, s=s: e.matmul(
                        pA[:, h * H:(h + 1) * H],
                        warena[:, W1OFF + s * 2048 + kc * 128: W1OFF + s * 2048 + (kc + 1) * 128],
                        u[:, kc * TC + h * H: kc * TC + (h + 1) * H],
                        start=(kc == 0), stop=False),
                        deps=(ldw1,) + (prev_pA_consumed if kc == 0 else ())))
            for c in range(NEC):
                for h in range(2):
                    mmsA.append(E("T", lambda e, c=c, h=h, s=s: e.matmul(
                        pA[:, h * H:(h + 1) * H],
                        sb1[:, s * NEC * 128 + c * 128: s * NEC * 128 + (c + 1) * 128],
                        ma10[:, c * TC + h * H: c * TC + (h + 1) * H],
                        start=False, stop=(c == NEC - 1)), deps=(ldb1, a1_done[c])))
            for kc in range(DCH):
                for h in range(2):
                    mmsB.append(E("T", lambda e, kc=kc, h=h, s=s: e.matmul(
                        pB[:, h * H:(h + 1) * H],
                        warena[:, W3OFF + s * 2048 + kc * 128: W3OFF + s * 2048 + (kc + 1) * 128],
                        u[:, kc * TC + h * H: kc * TC + (h + 1) * H],
                        start=(kc == 0), stop=False),
                        deps=(ldw3,) + (prev_pB_consumed if kc == 0 else ())))
            for c in range(NEC):
                for h in range(2):
                    mmsB.append(E("T", lambda e, c=c, h=h, s=s: e.matmul(
                        pB[:, h * H:(h + 1) * H],
                        sb3[:, s * NEC * 128 + c * 128: s * NEC * 128 + (c + 1) * 128],
                        ma30[:, c * TC + h * H: c * TC + (h + 1) * H],
                        start=False, stop=(c == NEC - 1)), deps=(ldb3, a3_done[c])))

            accslot = acc[:, fc * TC:(fc + 1) * TC]
            spare = acc[:, NF * TC:(NF + 1) * TC]
            acc_guard = prev_accslot.pop(fc, ())
            spare_guard = prev_accslot.pop("spare", ())
            s0 = E("A", lambda e, spare=spare: e.activation(out=spare, in_=pA[:], func=AF.Silu),
                   deps=(mmsA[-1],) + spare_guard)
            x20 = E("V", lambda e, accslot=accslot, spare=spare: e.tensor_tensor(
                out=accslot, in0=spare, in1=pB[:], op=ALU.mult),
                deps=(s0, mmsB[-1]) + acc_guard)
            sc20 = E("V", lambda e, accslot=accslot, spare=spare: e.tensor_tensor(
                out=spare, in0=accslot, in1=rw0b[:], op=ALU.mult), deps=(x20, rw_ev[0], s0))
            a2mm = []
            for c in range(NEC):
                for h in range(2):
                    a2mm.append(E("T", lambda e, c=c, h=h, s=s, fc=fc, spare=spare: e.matmul(
                        pE[:, c * TC + h * H: c * TC + (h + 1) * H],
                        sa2b[:, s * ER + c * 128: s * ER + (c + 1) * 128],
                        spare[:, h * H:(h + 1) * H],
                        start=(fc == 0), stop=(fc == NF - 1)),
                        deps=(lda2, sc20) + (prev_pE_guard if fc == 0 else ())))
            a2_mm_last = a2mm[-1]
            mmdA, mmdB = [], []
            for c in range(NEC):
                for h in range(2):
                    mmdA.append(E("T", lambda e, c=c, h=h, s=s: e.matmul(
                        pA[:, h * H:(h + 1) * H],
                        sb1[:, s * NEC * 128 + c * 128: s * NEC * 128 + (c + 1) * 128],
                        da1[:, c * TC + h * H: c * TC + (h + 1) * H],
                        start=False, stop=(c == NEC - 1), skip_group_check=True), deps=(s0, ldb1)))
                    mmdB.append(E("T", lambda e, c=c, h=h, s=s: e.matmul(
                        pB[:, h * H:(h + 1) * H],
                        sb3[:, s * NEC * 128 + c * 128: s * NEC * 128 + (c + 1) * 128],
                        da3[:, c * TC + h * H: c * TC + (h + 1) * H],
                        start=False, stop=(c == NEC - 1), skip_group_check=True), deps=(x20, ldb3)))
            s1 = E("A", lambda e, accslot=accslot: e.activation(out=accslot, in_=pA[:], func=AF.Silu),
                   deps=(mmdA[-1], sc20))
            x21 = E("V", lambda e, accslot=accslot: e.tensor_tensor(
                out=accslot, in0=accslot, in1=pB[:], op=ALU.mult),
                deps=(s1, mmdB[-1]))
            sc21 = E("V", lambda e, accslot=accslot: e.tensor_tensor(
                out=accslot, in0=accslot, in1=rw1b[:], op=ALU.mult),
                deps=(x21, rw_ev[1]))
            accw = E("V", lambda e, accslot=accslot, spare=spare: e.tensor_tensor(
                out=accslot, in0=spare, in1=accslot, op=ALU.add),
                deps=(sc21, sc20) + tuple(a2mm))
            prev_accslot["spare"] = (accw,)
            acc_done.append(accw)
            w_last_use[s] = (s1, x21) + tuple(a2mm) + tuple(mmdA) + tuple(mmdB)
            prev_pA_consumed = (s1,)
            prev_pB_consumed = (x21,)

        ev_a2p0 = []
        for c in range(NEC):
            ev_a2p0.append(E("V", lambda e, c=c: e.tensor_copy(
                out=a2p0(c), in_=pE[:, c * TC:(c + 1) * TC]),
                deps=(a2_mm_last,) + tuple(u_done.values() if False else ())))
        mb2_ev = {}
        for m in range(2):
            mm = []
            for c in range(NEC):
                for h in range(2):
                    mm.append(E("T", lambda e, m=m, c=c, h=h: e.matmul(
                        pA[:, h * H:(h + 1) * H] if c == 0 else pB[:, h * H:(h + 1) * H],
                        ssel[32 * (m + 1):32 * (m + 1) + 8, c * 128:(c + 1) * 128], lgsb[32 * (m + 1):32 * (m + 1) + 8, h * H:(h + 1) * H],
                        start=True, stop=True),
                        deps=(prev_pA_consumed + prev_pB_consumed if m == 0 else tuple(mb2_ev.values()))))
            for c in range(NEC):
                mb2_ev[(m, c)] = E("V", lambda e, m=m, c=c: e.tensor_copy(
                    out=mbigp(m, c), in_=(pA[:] if c == 0 else pB[:])), deps=tuple(mm) + tuple(ev_a2p0))
        a2acc_mm = []
        last_by_slot = {}
        for fc in range(NF):
            s = fc % 2
            dep_w = last_by_slot.pop(s, None)
            lda2 = E(f"dP:a2{s}", lambda e, fc=fc, s=s: e.dma_start(
                out=sa2b[:, s * ER:(s + 1) * ER], in_=a2R[fc]),
                deps=(dep_w,) if dep_w else (a2_mm_last,))
            fc_mm = []
            for c in range(NEC):
                for h in range(2):
                    fc_mm.append(E("T", lambda e, fc=fc, s=s, c=c, h=h: e.matmul(
                        pE[:, c * TC + h * H: c * TC + (h + 1) * H],
                        sa2b[:, s * ER + c * 128: s * ER + (c + 1) * 128],
                        acc[:, fc * TC + h * H: fc * TC + (h + 1) * H],
                        start=(fc == 0), stop=(fc == NF - 1)),
                        deps=(lda2, acc_done[fc]) + ((ev_a2p0[-1],) if fc == 0 else ())))
            a2acc_mm.extend(fc_mm)
            last_by_slot[s] = fc_mm[-1]
        msk2 = []
        for c in range(NEC):
            m0 = E("V", lambda e, c=c: e.tensor_tensor(
                out=ma10[:, c * TC:(c + 1) * TC], in0=a2p0(c), in1=mbigp(0, c), op=ALU.mult),
                deps=(ev_a2p0[c], mb2_ev[(0, c)]) + (a1_done[c],))
            t = E("V", lambda e, c=c: e.tensor_tensor(
                out=da1[:, c * TC:(c + 1) * TC], in0=pE[:, c * TC:(c + 1) * TC], in1=a2p0(c),
                op=ALU.subtract), deps=(a2acc_mm[-1], ev_a2p0[c]))
            m1_ = E("V", lambda e, c=c: e.tensor_tensor(
                out=ma30[:, c * TC:(c + 1) * TC], in0=da1[:, c * TC:(c + 1) * TC], in1=mbigp(1, c),
                op=ALU.mult), deps=(t, mb2_ev[(1, c)], a3_done[c]))
            msk2.extend((m0, m1_))

        out_dmas = []
        pDn = (pA, pB)
        prev_dn = {0: prev_pA_consumed + tuple(mb2_ev.values()), 1: prev_pB_consumed + tuple(mb2_ev.values())}
        w2_last = {}
        b2_last = {}
        for mt in range(16):
            par = mt % 2
            pD = pDn[par]
            ldb2 = E(f"dP:b2{par}", lambda e, mt=mt, par=par: e.dma_start(
                out=sb2s[:, par * NEC * 128:(par + 1) * NEC * 128].rearrange("p (c f) -> p c f", c=NEC),
                in_=b2M[mt].rearrange("c p f -> p c f")), deps=b2_last.pop(par, (a2_mm_last,)))
            dn_mm = []
            for piece in range(2):
                wsl = (2 * mt + piece) % 2
                wslot = wsl * ((NF + 1) // 2) * 128
                n_k = (NF + 1) // 2 if piece == 0 else NF - (NF + 1) // 2
                k0 = 0 if piece == 0 else (NF + 1) // 2
                ldw = E(f"dP:w2{wsl}", lambda e, mt=mt, piece=piece, k0=k0, n_k=n_k, wslot=wslot: e.dma_start(
                    out=warena[:, wslot: wslot + n_k * 128].rearrange("p (c f) -> p c f", c=n_k),
                    in_=w2M[mt, k0:k0 + n_k].rearrange("c p f -> p c f")),
                    deps=w2_last.pop(wsl, (acc_done[-1],) if mt < 1 else ()))
                for kc in range(n_k):
                    for h in range(2):
                        dn_mm.append(E("T", lambda e, kc=kc, k0=k0, h=h, wslot=wslot, pD=pD: e.matmul(
                            pD[:, h * H:(h + 1) * H],
                            warena[:, wslot + kc * 128: wslot + (kc + 1) * 128],
                            acc[:, (k0 + kc) * TC + h * H: (k0 + kc) * TC + (h + 1) * H],
                            start=(k0 + kc == 0), stop=False),
                            deps=(ldw, acc_done[k0 + kc]) + (prev_dn[par] if (k0 + kc) == 0 else ())))
                w2_last[wsl] = (dn_mm[-1],)
            for c in range(NEC):
                for src_i, src in ((0, ma10), (1, ma30)):
                    for h in range(2):
                        last = (c == NEC - 1 and src_i == 1)
                        dn_mm.append(E("T", lambda e, c=c, src=src, h=h, par=par, pD=pD, last=last: e.matmul(
                            pD[:, h * H:(h + 1) * H],
                            sb2s[:, par * NEC * 128 + c * 128: par * NEC * 128 + (c + 1) * 128],
                            src[:, c * TC + h * H: c * TC + (h + 1) * H],
                            start=False, stop=last), deps=(ldb2,) + tuple(msk2)))
            b2_last[par] = (dn_mm[-1],)
            for h in range(2):
                bslot = h
                deps = xst_users.pop(bslot, ())

                ldx = E(f"dS:x{bslot}", lambda e, mt=mt, h=h, b=bslot: e.dma_start(
                    out=xst[:, b * H:(b + 1) * H], in_=x[mt, :, h * H:(h + 1) * H]), deps=deps)
                ev = E("V", lambda e, h=h, b=bslot, pD=pD: e.scalar_tensor_tensor(
                    out=xst[:, b * H:(b + 1) * H], in0=pD[:, h * H:(h + 1) * H], scalar=out_scale,
                    in1=xst[:, b * H:(b + 1) * H], op0=ALU.mult, op1=ALU.add),
                    deps=(ldx, dn_mm[-1]))
                st = E(f"dS:o{bslot}", lambda e, mt=mt, h=h, b=bslot: e.dma_start(
                    out=xo[mt, :, h * H:(h + 1) * H], in_=xst[:, b * H:(b + 1) * H]), deps=(ev,))
                out_dmas.append(st)
                xst_users[bslot] = (st,)
            prev_dn[par] = (out_dmas[-1], out_dmas[-2])

        p.finish(out_dmas)
        p.build(es)
    return nc, dt_in



def _c(a):
    return np.ascontiguousarray(a)


def prep_ffn_weights(Wg, W1, W2, W3, A1, B1, A2, B2, A3, B3, R, sel_val):
    ER = 8 * R
    F = W1.shape[0]
    NF = F // 128
    NEC = ER // 128
    d = {}
    d["wgT"] = _c(Wg.T.reshape(DCH, 128, 8))
    d["w1R"] = _c(W1.T.reshape(DCH, 128, NF, 128).transpose(2, 0, 1, 3))
    d["w3R"] = _c(W3.T.reshape(DCH, 128, NF, 128).transpose(2, 0, 1, 3))
    d["w2M"] = _c(W2.T.reshape(NF, 128, DCH, 128).transpose(2, 0, 1, 3))
    d["a1T"] = _c(A1.reshape(ER, D).T.reshape(DCH, 128, ER))
    d["a3T"] = _c(A3.reshape(ER, D).T.reshape(DCH, 128, ER))
    d["a2R"] = _c(A2.reshape(ER, F).T.reshape(NF, 128, ER))
    d["b1R"] = _c(B1.transpose(0, 2, 1).reshape(NEC, 128, NF, 128).transpose(2, 0, 1, 3))
    d["b3R"] = _c(B3.transpose(0, 2, 1).reshape(NEC, 128, NF, 128).transpose(2, 0, 1, 3))
    d["b2M"] = _c(B2.transpose(0, 2, 1).reshape(NEC, 128, DCH, 128).transpose(2, 0, 1, 3))
    sel = np.zeros((8, ER), np.float32)
    for e in range(8):
        sel[e, e * R:(e + 1) * R] = sel_val
    d["ssel"] = sel
    d["onesc"] = np.ones((128, 1), np.float32)
    d["onesf"] = np.ones((1, 128), np.float32)
    d["identf"] = np.eye(128, dtype=np.float32)
    return d


def x_to_cores(xfull):
    out = []
    for c in range(NCORE):
        b, q = c // 4, c % 4
        sl = xfull[b, q * TC:(q + 1) * TC, :]
        out.append(_c(sl.T.reshape(DCH, 128, TC)))
    return out


def x_from_cores(parts):
    xf = np.empty((2, 4096, D), np.float32)
    for c in range(NCORE):
        b, q = c // 4, c % 4
        xf[b, q * TC:(q + 1) * TC, :] = parts[c].reshape(D, TC).T
    return xf



PI = float(np.pi)
TWO_PI = float(2 * np.pi)


def emit_ln(p, E, x, u, pA, pB, pE, xst, rows2, onesc, onesc_b, onesf, dep0=()):
    H = 1024 // 2
    nmean = rows2[0:1, 0:TC]
    rstdr = rows2[0:1, TC:2 * TC]
    xld = {}
    xst_users = {}
    sum_mm = []
    sq_mm = []
    prev_u_write = {}
    for i in range(DCH):
        for h in range(2):
            b = (2 * i + h) % 2
            deps = xst_users.pop(b, dep0)
            xld[(0, i, h)] = E(f"dS:x{b}", lambda e, i=i, h=h, b=b: e.dma_start(
                out=xst[:, b * H:(b + 1) * H], in_=x[i, :, h * H:(h + 1) * H]), deps=deps)
            m1 = E("T", lambda e, i=i, h=h, b=b: e.matmul(
                pE[0:1, h * H:(h + 1) * H], onesc[:], xst[:, b * H:(b + 1) * H],
                start=(i == 0), stop=(i == DCH - 1)), deps=(xld[(0, i, h)],) + dep0)
            sum_mm.append(m1)
            sq = E("A", lambda e, i=i, h=h, b=b: e.activation(
                out=u[:, i * TC + h * H: i * TC + (h + 1) * H], in_=xst[:, b * H:(b + 1) * H],
                func=AF.Square), deps=(xld[(0, i, h)],))
            m2 = E("T", lambda e, i=i, h=h: e.matmul(
                pE[32:33, h * H:(h + 1) * H], onesc_b[:], u[:, i * TC + h * H: i * TC + (h + 1) * H],
                start=(i == 0), stop=(i == DCH - 1)), deps=(sq,) + dep0)
            sq_mm.append(m2)
            xst_users[b] = (m1, sq)
            prev_u_write[(i, h)] = m2
    t_nm = E("V", lambda e: e.tensor_scalar(out=nmean, in0=pE[0:1, 0:TC], scalar1=-1.0 / D,
                                            scalar2=None, op0=ALU.mult), deps=(sum_mm[-1],))
    rowB = xst[0:1, 0:TC]
    t_msq = E("V", lambda e: e.tensor_scalar(out=rowB, in0=pE[32:33, 0:TC], scalar1=1.0 / D,
                                             scalar2=None, op0=ALU.mult),
              deps=(sq_mm[-1],) + tuple(xst_users[0]) + tuple(xst_users[1]))
    t_e2 = E("V", lambda e: e.tensor_tensor(out=rstdr, in0=nmean, in1=nmean, op=ALU.mult),
             deps=(t_nm,))
    t_var = E("V", lambda e: e.tensor_tensor(out=rowB, in0=rowB, in1=rstdr, op=ALU.subtract),
              deps=(t_msq, t_e2))
    t_vE = E("V", lambda e: e.tensor_scalar(out=rowB, in0=rowB, scalar1=EPS_LN, scalar2=None,
                                            op0=ALU.add), deps=(t_var,))
    t_ln = E("A", lambda e: e.activation(out=rowB, in_=rowB, func=AF.Ln), deps=(t_vE,))
    t_rstd = E("A", lambda e: e.activation(out=rstdr, in_=rowB, func=AF.Exp, scale=-0.5),
               deps=(t_ln, t_e2))
    bc_mm = []
    for h in range(2):
        bc_mm.append(E("T", lambda e, h=h: e.matmul(
            pA[:, h * H:(h + 1) * H], onesf[:], nmean[0:1, h * H:(h + 1) * H],
            start=True, stop=True), deps=(t_nm,)))
    for h in range(2):
        bc_mm.append(E("T", lambda e, h=h: e.matmul(
            pB[:, h * H:(h + 1) * H], onesf[:], rstdr[0:1, h * H:(h + 1) * H],
            start=True, stop=True), deps=(t_rstd,)))
    u_done = {}
    for i in range(DCH):
        for h in range(2):
            b = (2 * i + h) % 2
            deps = xst_users.pop(b, ())
            if i == 0:
                deps = deps + (t_msq, t_var, t_vE, t_ln, t_rstd)
            ld = E(f"dS:x{b}", lambda e, i=i, h=h, b=b: e.dma_start(
                out=xst[:, b * H:(b + 1) * H], in_=x[i, :, h * H:(h + 1) * H]), deps=deps)
            t1 = E("V", lambda e, b=b, h=h: e.tensor_tensor(
                out=xst[:, b * H:(b + 1) * H], in0=xst[:, b * H:(b + 1) * H],
                in1=pA[:, h * H:(h + 1) * H], op=ALU.add), deps=(ld, bc_mm[h]))
            t2 = E("V", lambda e, i=i, h=h, b=b: e.tensor_tensor(
                out=u[:, i * TC + h * H: i * TC + (h + 1) * H], in0=xst[:, b * H:(b + 1) * H],
                in1=pB[:, h * H:(h + 1) * H], op=ALU.mult),
                deps=(t1, bc_mm[2 + h], prev_u_write[(i, h)]))
            u_done[(i, h)] = t2
            xst_users[b] = (t2,)
    return u_done, xst_users, (t_nm, t_rstd)


def emit_sincos(E, src, sin_out, cos_out, scr1, scr2, scri, deps):
    outs = []
    for phase_off, dst in ((0.0, sin_out), (PI / 2.0, cos_out)):
        t0 = E("V", lambda e, po=phase_off: e.tensor_scalar(
            out=scr1, in0=src, scalar1=1.0 / TWO_PI, scalar2=(po / TWO_PI),
            op0=ALU.mult, op1=ALU.add), deps=deps if not outs else (deps + (outs[-1],)))
        t1 = E("V", lambda e: e.tensor_copy(out=scri, in_=scr1), deps=(t0,))
        t2 = E("V", lambda e: e.tensor_copy(out=scr2, in_=scri), deps=(t1,))
        t3 = E("V", lambda e: e.tensor_tensor(out=scr1, in0=scr1, in1=scr2, op=ALU.subtract),
               deps=(t2, t0))
        t4 = E("V", lambda e: e.tensor_scalar(out=scr1, in0=scr1, scalar1=TWO_PI, scalar2=None,
                                              op0=ALU.mult), deps=(t3,))
        c1 = E("V", lambda e: e.tensor_scalar(out=scr2, in0=scr1, scalar1=PI, scalar2=-TWO_PI,
                                              op0=ALU.is_gt, op1=ALU.mult), deps=(t4,))
        t5 = E("V", lambda e: e.tensor_tensor(out=scr1, in0=scr1, in1=scr2, op=ALU.add),
               deps=(c1,))
        c2 = E("V", lambda e: e.tensor_scalar(out=scr2, in0=scr1, scalar1=-PI, scalar2=TWO_PI,
                                              op0=ALU.is_lt, op1=ALU.mult), deps=(t5,))
        t6 = E("V", lambda e: e.tensor_tensor(out=scr1, in0=scr1, in1=scr2, op=ALU.add),
               deps=(c2,))
        t7 = E("A", lambda e, dst=dst: e.activation(out=dst, in_=scr1, func=AF.Sin), deps=(t6,))
        outs.append(t7)
    return outs[0], outs[1]


def emit_l0_scalars(E, cols, lamre_u, lamim, logdel, onescol32, scr, deps0=()):
    lr, li, ar, ai, wr, wi = (cols[k] for k in ("lr", "li", "ar", "ai", "wr", "wi"))
    s0, s1, s2, s3 = scr[:, 0:1], scr[:, 1:2], scr[:, 2:3], scr[:, 3:4]
    si = cols["scri"]
    tk = {}
    t1 = E("A", lambda e: e.activation(out=s0, in_=lamre_u, func=AF.Abs), deps=deps0)
    t2 = E("A", lambda e: e.activation(out=s0, in_=s0, func=AF.Exp, scale=-1.0), deps=(t1,))
    t3 = E("A", lambda e: e.activation(out=s0, in_=s0, func=AF.Ln, bias=onescol32), deps=(t2,))
    t4 = E("V", lambda e: e.tensor_scalar(out=s1, in0=lamre_u, scalar1=0.0, scalar2=None,
                                          op0=ALU.max), deps=deps0)
    t5 = E("V", lambda e: e.tensor_tensor(out=s0, in0=s0, in1=s1, op=ALU.add), deps=(t3, t4))
    t6 = E("V", lambda e: e.tensor_scalar(out=s1, in0=s0, scalar1=-1.0, scalar2=-(1e-4 + 0.01),
                                          op0=ALU.mult, op1=ALU.add), deps=(t5,))
    t7 = E("A", lambda e: e.activation(out=s2, in_=logdel, func=AF.Exp), deps=deps0)
    t8 = E("V", lambda e: e.tensor_tensor(out=lr, in0=s1, in1=s2, op=ALU.mult), deps=(t6, t7))
    t9 = E("V", lambda e: e.tensor_tensor(out=li, in0=lamim, in1=s2, op=ALU.mult), deps=(t7,) + deps0)
    t10 = E("A", lambda e: e.activation(out=s2, in_=lr, func=AF.Exp), deps=(t8, t9))
    snt, cst = emit_sincos(E, li, s0, s3, scr[:, 4:5], scr[:, 5:6], si, (t9, t6, t5, t3, t2))
    t11 = E("V", lambda e: e.tensor_tensor(out=ar, in0=s2, in1=s3, op=ALU.mult), deps=(t10, cst))
    t12 = E("V", lambda e: e.tensor_tensor(out=ai, in0=s2, in1=s0, op=ALU.mult), deps=(t10, snt, t11))
    n1 = E("V", lambda e: e.tensor_tensor(out=s2, in0=s1, in1=s1, op=ALU.mult), deps=(t6, t10, t11, t12))
    n2 = E("V", lambda e: e.tensor_tensor(out=s3, in0=lamim, in1=lamim, op=ALU.mult), deps=(cst, t11, t12))
    n3 = E("V", lambda e: e.tensor_tensor(out=s2, in0=s2, in1=s3, op=ALU.add), deps=(n1, n2))
    n4 = E("V", lambda e: e.reciprocal(out=s2, in_=s2), deps=(n3,))
    a1 = E("V", lambda e: e.tensor_scalar(out=s3, in0=ar, scalar1=1.0, scalar2=None,
                                          op0=ALU.subtract), deps=(t11, n2, n3))
    q1 = E("V", lambda e: e.tensor_tensor(out=s0, in0=s3, in1=s1, op=ALU.mult), deps=(a1, snt, t12))
    q2 = E("V", lambda e: e.tensor_tensor(out=wr, in0=ai, in1=lamim, op=ALU.mult), deps=(t12,))
    q3 = E("V", lambda e: e.tensor_tensor(out=wr, in0=wr, in1=s0, op=ALU.add), deps=(q1, q2))
    q4 = E("V", lambda e: e.tensor_tensor(out=wr, in0=wr, in1=s2, op=ALU.mult), deps=(q3, n4))
    q5 = E("V", lambda e: e.tensor_tensor(out=s0, in0=ai, in1=s1, op=ALU.mult), deps=(t12, q1, q3))
    q6 = E("V", lambda e: e.tensor_tensor(out=wi, in0=s3, in1=lamim, op=ALU.mult), deps=(a1,))
    q7 = E("V", lambda e: e.tensor_tensor(out=wi, in0=s0, in1=wi, op=ALU.subtract), deps=(q5, q6))
    q8 = E("V", lambda e: e.tensor_tensor(out=wi, in0=wi, in1=s2, op=ALU.mult), deps=(q7, q4))
    tk["done"] = (q4, q8, t12, t11, t8, t9)
    return tk


def build_l0a():
    nc = bass.Bass("TRN2", target_bir_lowering=False)

    def din(name, shape, dt=F32):
        return nc.dram_tensor(name, shape, dt, kind="ExternalInput").ap()

    x = din("x", [DCH, 128, TC])
    btre = din("btre", [DCH, 128, 32])
    btim = din("btim", [DCH, 128, 32])
    lamre_d = din("lamre", [32, 1])
    lamim_d = din("lamim", [32, 1])
    logdel_d = din("logdel", [32, 1])
    iota256_d = din("iota256", [32, 256])
    rstm_d = din("rstm", [64, TC])
    onesc_d = din("onesc", [128, 1])
    onesf_d = din("onesf", [1, 128])
    xloc = nc.dram_tensor("xloc", [64, TC], F32, kind="ExternalOutput").ap()
    u_out = nc.dram_tensor("u_out", [DCH, 128, TC], BF16, kind="ExternalOutput").ap()

    p = Prog(nc)
    es = ExitStack()
    with es:
        sb = lambda n, shape, dt: es.enter_context(nc.sbuf_tensor(n, shape, dt))
        u = sb("u", [128, DCH * TC], BF16)
        xst = sb("xst", [128, TC], F32)
        rows2 = sb("rows2", [1, 2 * TC], F32)
        sbt = sb("sbt", [128, 2 * 64], BF16)
        cols = sb("cols", [32, 20], F32)
        coli = sb("coli", [32, 1], I32)
        tabs = sb("tabs", [32, 8 * 256], F32)
        tp1 = sb("tp1", [32, 2 * 256], F32)
        tabt = sb("tabt", [32, 2 * TC], F32)
        busc_re = sb("busc_re", [32, TC], F32)
        busc_im = sb("busc_im", [32, TC], F32)
        vsc_re = sb("vsc_re", [32, TC], F32)
        vsc_im = sb("vsc_im", [32, TC], F32)
        XL_re = sb("XL_re", [32, TC], F32)
        XL_im = sb("XL_im", [32, TC], F32)
        rstm = sb("rstm_s", [32, TC], F32)
        iota256 = sb("iota256_s", [32, 256], F32)
        onesc = sb("s_onesc", [128, 1], F32)
        onesc_b = sb("s_onesc_b", [128, 1], BF16)
        onesf = sb("s_onesf", [1, 128], F32)
        pA = es.enter_context(nc.psum_tensor("pA", [128, TC], F32))
        pB = es.enter_context(nc.psum_tensor("pB", [128, TC], F32))
        pE = es.enter_context(nc.psum_tensor("pE", [128, TC], F32))

        E = p.emit
        H = 512
        c0 = E("dA:c", lambda e: e.dma_start(out=onesc[:], in_=onesc_d))
        c1 = E("dP:c", lambda e: e.dma_start(out=onesc_b[:], in_=onesc_d))
        c2 = E("dA:c", lambda e: e.dma_start(out=onesf[:], in_=onesf_d), deps=(c0,))
        c3 = E("dA:c", lambda e: e.dma_start(out=cols[:, 8:9], in_=lamre_d), deps=(c2,))
        c4 = E("dA:c", lambda e: e.dma_start(out=cols[:, 9:10], in_=lamim_d), deps=(c3,))
        c5 = E("dA:c", lambda e: e.dma_start(out=cols[:, 10:11], in_=logdel_d), deps=(c4,))
        c6 = E("dA:c", lambda e: e.dma_start(out=iota256[:], in_=iota256_d), deps=(c5,))
        c7 = E("dA:c", lambda e: e.dma_start(out=rstm[:], in_=rstm_d[0:32]), deps=(c6,))

        u_done, xst_users, _ = emit_ln(p, E, x, u, pA, pB, pE, xst, rows2, onesc, onesc_b,
                                       onesf, dep0=(c0, c1))

        ccols = {"lr": cols[:, 0:1], "li": cols[:, 1:2], "ar": cols[:, 2:3], "ai": cols[:, 3:4],
                 "wr": cols[:, 4:5], "wi": cols[:, 5:6], "scri": coli[:, 0:1]}
        tk = emit_l0_scalars(E, ccols, cols[:, 8:9], cols[:, 9:10], cols[:, 10:11],
                             onesc[0:32, 0:1], cols[:, 11:18], deps0=(c3, c4, c5, c0))
        sdone = tk["done"]

        EM, EP, SN, CS, T1, T2 = (tabs[:, k * 256:(k + 1) * 256] for k in range(6))
        ph1 = E("V", lambda e: e.tensor_scalar(out=T1, in0=iota256[:], scalar1=ccols["lr"],
                                               scalar2=None, op0=ALU.mult), deps=sdone + (c6,))
        e1 = E("A", lambda e: e.activation(out=EM, in_=T1, func=AF.Exp, scale=-1.0), deps=(ph1,))
        e2 = E("A", lambda e: e.activation(out=EP, in_=T1, func=AF.Exp), deps=(ph1,))
        ph2 = E("V", lambda e: e.tensor_scalar(out=T2, in0=iota256[:], scalar1=ccols["li"],
                                               scalar2=None, op0=ALU.mult), deps=sdone + (c6,))
        coli2 = sb("coli2", [32, 256], I32)
        snt, cst = emit_sincos(E, T2, SN, CS, tabs[:, 6 * 256:7 * 256], tabs[:, 7 * 256:8 * 256],
                               coli2[:], (ph2,))
        g1 = E("V", lambda e: e.tensor_tensor(out=T1, in0=EP, in1=CS, op=ALU.mult),
               deps=(e2, cst, e1))
        g2 = E("V", lambda e: e.tensor_tensor(out=T2, in0=EP, in1=SN, op=ALU.mult),
               deps=(e2, snt, g1))
        h1 = E("V", lambda e: e.tensor_scalar(out=tp1[:, 0:256], in0=T1, scalar1=ccols["ar"],
                                              scalar2=None, op0=ALU.mult), deps=(g1,))
        h2 = E("V", lambda e: e.tensor_scalar(out=tp1[:, 256:512], in0=T2, scalar1=ccols["ai"],
                                              scalar2=None, op0=ALU.mult), deps=(g2,))
        h3 = E("V", lambda e: e.tensor_tensor(out=tp1[:, 0:256], in0=tp1[:, 0:256],
                                              in1=tp1[:, 256:512], op=ALU.subtract), deps=(h1, h2))
        h4 = E("V", lambda e: e.tensor_scalar(out=tp1[:, 256:512], in0=T1, scalar1=ccols["ai"],
                                              scalar2=None, op0=ALU.mult), deps=(h3, g1))
        h5 = E("V", lambda e: e.tensor_scalar(out=T1, in0=T2, scalar1=ccols["ar"],
                                              scalar2=None, op0=ALU.mult), deps=(h3, h4, g2))
        h6 = E("V", lambda e: e.tensor_tensor(out=tp1[:, 256:512], in0=tp1[:, 256:512],
                                              in1=T1, op=ALU.add), deps=(h4, h5))

        bu_mm = []
        last_slot = {}
        for kc in range(DCH):
            sl = kc % 2
            dep_w = last_slot.pop(sl, (c1,))
            ld = E(f"dP:bt{sl}", lambda e, kc=kc, sl=sl: e.dma_start(
                out=sbt[:, sl * 64:sl * 64 + 32], in_=btre[kc]), deps=dep_w)
            ld2 = E(f"dP:bt{sl}", lambda e, kc=kc, sl=sl: e.dma_start(
                out=sbt[:, sl * 64 + 32:sl * 64 + 64], in_=btim[kc]), deps=(ld,))
            mm = []
            for h in range(2):
                mm.append(E("T", lambda e, kc=kc, sl=sl, h=h: e.matmul(
                    pA[0:32, h * H:(h + 1) * H], sbt[:, sl * 64:sl * 64 + 32],
                    u[:, kc * TC + h * H: kc * TC + (h + 1) * H],
                    start=(kc == 0), stop=(kc == DCH - 1)),
                    deps=(ld, u_done[(kc, h)]) + (xst_users[0] + xst_users[1] if kc == 0 else ())))
                mm.append(E("T", lambda e, kc=kc, sl=sl, h=h: e.matmul(
                    pB[0:32, h * H:(h + 1) * H], sbt[:, sl * 64 + 32:sl * 64 + 64],
                    u[:, kc * TC + h * H: kc * TC + (h + 1) * H],
                    start=(kc == 0), stop=(kc == DCH - 1)), deps=(ld2, u_done[(kc, h)])))
            bu_mm.extend(mm)
            last_slot[sl] = tuple(mm)
        s1 = E("V", lambda e: e.tensor_scalar(out=busc_re[:], in0=pA[0:32, 0:TC],
                                              scalar1=ccols["wr"], scalar2=None, op0=ALU.mult),
               deps=(bu_mm[-1],) + sdone)
        s2_ = E("V", lambda e: e.tensor_scalar(out=vsc_re[:], in0=pB[0:32, 0:TC],
                                               scalar1=ccols["wi"], scalar2=None, op0=ALU.mult),
                deps=(bu_mm[-1],) + sdone)
        s3_ = E("V", lambda e: e.tensor_tensor(out=busc_re[:], in0=busc_re[:],
                                               in1=vsc_re[:], op=ALU.subtract), deps=(s1, s2_))
        s4 = E("V", lambda e: e.tensor_scalar(out=busc_im[:], in0=pA[0:32, 0:TC],
                                              scalar1=ccols["wi"], scalar2=None, op0=ALU.mult),
               deps=(bu_mm[-1], s3_) + sdone)
        s5 = E("V", lambda e: e.tensor_scalar(out=vsc_im[:], in0=pB[0:32, 0:TC],
                                              scalar1=ccols["wr"], scalar2=None, op0=ALU.mult),
               deps=(bu_mm[-1], s2_) + sdone)
        s6 = E("V", lambda e: e.tensor_tensor(out=busc_im[:], in0=busc_im[:],
                                              in1=vsc_im[:], op=ALU.add), deps=(s4, s5))

        def tile4(dst, srcap, deps):
            toks = []
            for c in range(4):
                toks.append(E("V", lambda e, c=c, dst=dst, srcap=srcap: e.tensor_copy(
                    out=bass.AP(dst.tensor, dst.offset + c * 256, [list(dst.ap[0]), [1, 256]]),
                    in_=srcap), deps=tuple(deps) + tuple(toks[-1:])))
            return toks

        w1 = E("V", lambda e: e.tensor_tensor(out=T1, in0=EM, in1=CS, op=ALU.mult),
               deps=(e1, cst, h5, h6))
        w2 = E("V", lambda e: e.tensor_tensor(out=T2, in0=EM, in1=SN, op=ALU.mult),
               deps=(e1, snt, h6, w1))
        tA = tile4(tabt[:, 0:TC], T1, (w1,))
        tB = tile4(tabt[:, TC:2 * TC], T2, (w2,))
        v1 = E("V", lambda e: e.tensor_tensor(out=vsc_re[:], in0=busc_re[:],
                                              in1=tabt[:, 0:TC], op=ALU.mult),
               deps=(s3_, s5) + tuple(tA))
        v2 = E("V", lambda e: e.tensor_tensor(out=XL_re[:], in0=busc_im[:],
                                              in1=tabt[:, TC:2 * TC], op=ALU.mult),
               deps=(s6,) + tuple(tB))
        v3 = E("V", lambda e: e.tensor_tensor(out=vsc_re[:], in0=vsc_re[:],
                                              in1=XL_re[:], op=ALU.add), deps=(v1, v2))
        v4_ = E("V", lambda e: e.tensor_tensor(out=XL_im[:], in0=busc_re[:],
                                               in1=tabt[:, TC:2 * TC], op=ALU.mult),
                deps=(v3,) + tuple(tB))
        v5 = E("V", lambda e: e.tensor_tensor(out=vsc_im[:], in0=busc_im[:],
                                              in1=tabt[:, 0:TC], op=ALU.mult),
               deps=(s6, v2) + tuple(tA))
        v6 = E("V", lambda e: e.tensor_tensor(out=vsc_im[:], in0=vsc_im[:],
                                              in1=XL_im[:], op=ALU.subtract), deps=(v4_, v5))
        sc_r = E("V", lambda e: e.tensor_tensor_scan(out=vsc_re[:], data0=rstm[:], data1=vsc_re[:],
                                                     initial=0.0, op0=ALU.mult, op1=ALU.add),
                 deps=(v3, c7))
        sc = E("V", lambda e: e.tensor_tensor_scan(out=vsc_im[:], data0=rstm[:], data1=vsc_im[:],
                                                   initial=0.0, op0=ALU.mult, op1=ALU.add),
               deps=(v6, c7, sc_r))
        y1 = E("V", lambda e: e.tensor_tensor(out=T1, in0=EP, in1=CS, op=ALU.mult),
               deps=(sc, e2, cst) + tuple(tA))
        y2 = E("V", lambda e: e.tensor_tensor(out=T2, in0=EP, in1=SN, op=ALU.mult),
               deps=(sc, e2, snt, y1) + tuple(tB))
        tC = tile4(tabt[:, 0:TC], T1, (y1, v1, v5, sc))
        tD = tile4(tabt[:, TC:2 * TC], T2, (y2, v2, v4_, sc))
        x1 = E("V", lambda e: e.tensor_tensor(out=XL_re[:], in0=vsc_re[:],
                                              in1=tabt[:, 0:TC], op=ALU.mult),
               deps=(sc, v4_) + tuple(tC))
        x2 = E("V", lambda e: e.tensor_tensor(out=busc_re[:], in0=vsc_im[:],
                                              in1=tabt[:, TC:2 * TC], op=ALU.mult),
               deps=(sc, v1, v5) + tuple(tD))
        x3 = E("V", lambda e: e.tensor_tensor(out=XL_re[:], in0=XL_re[:],
                                              in1=busc_re[:], op=ALU.subtract), deps=(x1, x2))
        x4 = E("V", lambda e: e.tensor_tensor(out=XL_im[:], in0=vsc_re[:],
                                              in1=tabt[:, TC:2 * TC], op=ALU.mult),
               deps=(x3, x1) + tuple(tD))
        x5 = E("V", lambda e: e.tensor_tensor(out=busc_im[:], in0=vsc_im[:],
                                              in1=tabt[:, 0:TC], op=ALU.mult),
               deps=(x2,) + tuple(tC))
        x6 = E("V", lambda e: e.tensor_tensor(out=XL_im[:], in0=XL_im[:],
                                              in1=busc_im[:], op=ALU.add), deps=(x4, x5))
        prev = (x3, x6)
        for c in range(1, 4):
            base = c * 256
            hre = XL_re[:, base - 1:base]
            him = XL_im[:, base - 1:base]
            k1 = E("V", lambda e, base=base, hre=hre: e.tensor_scalar(
                out=busc_re[:, base:base + 256], in0=tp1[:, 0:256], scalar1=hre, scalar2=None,
                op0=ALU.mult), deps=prev + (h3,))
            k2 = E("V", lambda e, base=base, him=him: e.tensor_scalar(
                out=vsc_re[:, base:base + 256], in0=tp1[:, 256:512], scalar1=him, scalar2=None,
                op0=ALU.mult), deps=prev + (h6, k1))
            k3 = E("V", lambda e, base=base: e.tensor_tensor(
                out=busc_re[:, base:base + 256], in0=busc_re[:, base:base + 256],
                in1=vsc_re[:, base:base + 256], op=ALU.subtract), deps=(k1, k2))
            k4 = E("V", lambda e, base=base, him=him: e.tensor_scalar(
                out=busc_im[:, base:base + 256], in0=tp1[:, 0:256], scalar1=him, scalar2=None,
                op0=ALU.mult), deps=prev + (k3,))
            k5 = E("V", lambda e, base=base, hre=hre: e.tensor_scalar(
                out=vsc_im[:, base:base + 256], in0=tp1[:, 256:512], scalar1=hre, scalar2=None,
                op0=ALU.mult), deps=prev + (k2,))
            k6 = E("V", lambda e, base=base: e.tensor_tensor(
                out=busc_im[:, base:base + 256], in0=busc_im[:, base:base + 256],
                in1=vsc_im[:, base:base + 256], op=ALU.add), deps=(k4, k5))
            k7 = E("V", lambda e, base=base: e.tensor_tensor(
                out=XL_re[:, base:base + 256], in0=XL_re[:, base:base + 256],
                in1=busc_re[:, base:base + 256], op=ALU.add), deps=(k3,))
            k8 = E("V", lambda e, base=base: e.tensor_tensor(
                out=XL_im[:, base:base + 256], in0=XL_im[:, base:base + 256],
                in1=busc_im[:, base:base + 256], op=ALU.add), deps=(k6,))
            prev = (k7, k8)

        o1a = E("dS:o0", lambda e: e.dma_start(out=xloc[0:32], in_=XL_re[:]), deps=prev)
        o1 = E("dS:o0", lambda e: e.dma_start(out=xloc[32:64], in_=XL_im[:]), deps=prev + (o1a,))
        o2 = E("dS:o1", lambda e: e.dma_start(
            out=u_out.rearrange("c p t -> p c t"), in_=u[:].rearrange("p (c t) -> p c t", c=DCH)),
            deps=tuple(u_done.values()))
        p.finish((o1, o2))
        p.build(es)
    return nc


def build_l0b():
    nc = bass.Bass("TRN2", target_bir_lowering=False)

    def din(name, shape, dt=F32):
        return nc.dram_tensor(name, shape, dt, kind="ExternalInput").ap()

    x = din("x", [DCH, 128, TC])
    u_in = din("u_in", [DCH, 128, TC], BF16)
    xloc_d = din("xloc", [64, TC])
    sprev_d = din("sprev", [64, 4])
    ct_d = din("ct", [64, DCH * 128])
    dcol_d = din("dcol", [128, DCH])
    lamre_d = din("lamre", [32, 1])
    lamim_d = din("lamim", [32, 1])
    logdel_d = din("logdel", [32, 1])
    iotap_d = din("iotap", [32, TC])
    onesc_d = din("onesc", [128, 1])
    xo = nc.dram_tensor("xo", [DCH, 128, TC], F32, kind="ExternalOutput").ap()

    p = Prog(nc)
    es = ExitStack()
    with es:
        sb = lambda n, shape, dt: es.enter_context(nc.sbuf_tensor(n, shape, dt))
        usb = sb("usb", [128, DCH * TC], BF16)
        XL_re = sb("XL_re", [32, TC], F32)
        XL_im = sb("XL_im", [32, TC], F32)
        Xb = sb("Xb", [64, TC], BF16)
        ctb = sb("ctb", [64, DCH * 128], BF16)
        dcols = sb("dcols", [128, DCH], F32)
        sprevR = sb("sprevR", [32, 4], F32)
        sprevI = sb("sprevI", [32, 4], F32)
        cols = sb("cols", [32, 20], F32)
        coli = sb("coli", [32, 1], I32)
        coli2 = sb("coli2", [32, TC], I32)
        tabs = sb("tabs", [32, 6 * TC], F32)
        iotap = sb("iotap_s", [32, TC], F32)
        xst = sb("xst", [128, TC], F32)
        utmp = sb("utmp", [128, TC], F32)
        onesc = sb("s_onesc", [128, 1], F32)
        pA = es.enter_context(nc.psum_tensor("pA", [128, TC], F32))
        pB = es.enter_context(nc.psum_tensor("pB", [128, TC], F32))

        E = p.emit
        H = 512
        c0 = E("dA:c", lambda e: e.dma_start(out=onesc[:], in_=onesc_d))
        c1 = E("dA:c", lambda e: e.dma_start(out=cols[:, 8:9], in_=lamre_d), deps=(c0,))
        c2 = E("dA:c", lambda e: e.dma_start(out=cols[:, 9:10], in_=lamim_d), deps=(c1,))
        c3 = E("dA:c", lambda e: e.dma_start(out=cols[:, 10:11], in_=logdel_d), deps=(c2,))
        c4 = E("dA:c", lambda e: e.dma_start(out=iotap[:], in_=iotap_d), deps=(c3,))
        c5a = E("dA:c", lambda e: e.dma_start(out=sprevR[:], in_=sprev_d[0:32]), deps=(c4,))
        c5 = E("dA:c", lambda e: e.dma_start(out=sprevI[:], in_=sprev_d[32:64]), deps=(c5a,))
        c6a = E("dA:c", lambda e: e.dma_start(out=XL_re[:], in_=xloc_d[0:32]), deps=(c5,))
        c6 = E("dA:c", lambda e: e.dma_start(out=XL_im[:], in_=xloc_d[32:64]), deps=(c6a,))
        c7 = E("dA:c", lambda e: e.dma_start(out=dcols[:], in_=dcol_d), deps=(c6,))
        c8 = E("dP:c", lambda e: e.dma_start(out=ctb[:], in_=ct_d))
        c9 = E("dP:c", lambda e: e.dma_start(
            out=usb[:].rearrange("p (c t) -> p c t", c=DCH),
            in_=u_in.rearrange("c p t -> p c t")), deps=(c8,))
        ng = E("V", lambda e: e.tensor_scalar(out=ctb[32:64, :], in0=ctb[32:64, :],
                                              scalar1=-1.0, scalar2=None, op0=ALU.mult), deps=(c8,))

        ccols = {"lr": cols[:, 0:1], "li": cols[:, 1:2], "ar": cols[:, 2:3], "ai": cols[:, 3:4],
                 "wr": cols[:, 4:5], "wi": cols[:, 5:6], "scri": coli[:, 0:1]}
        tk = emit_l0_scalars(E, ccols, cols[:, 8:9], cols[:, 9:10], cols[:, 10:11],
                             onesc[0:32, 0:1], cols[:, 11:18], deps0=(c1, c2, c3, c0))
        sdone = tk["done"]

        T1, T2, SN, CS = (tabs[:, k * TC:(k + 1) * TC] for k in range(4))
        ph1 = E("V", lambda e: e.tensor_scalar(out=T1, in0=iotap[:], scalar1=ccols["lr"],
                                               scalar2=None, op0=ALU.mult), deps=sdone + (c4,))
        em = E("A", lambda e: e.activation(out=T1, in_=T1, func=AF.Exp), deps=(ph1,))
        ph2 = E("V", lambda e: e.tensor_scalar(out=T2, in0=iotap[:], scalar1=ccols["li"],
                                               scalar2=None, op0=ALU.mult), deps=sdone + (c4,))
        snt, cst = emit_sincos(E, T2, SN, CS, tabs[:, 4 * TC:5 * TC],
                               tabs[:, 5 * TC:6 * TC], coli2[:], (ph2,))

        f1 = E("V", lambda e: e.tensor_tensor(out=T2, in0=T1, in1=CS, op=ALU.mult),
               deps=(em, cst))
        f2 = E("V", lambda e: e.tensor_tensor(out=CS, in0=T1, in1=SN, op=ALU.mult),
               deps=(em, snt, f1))
        hcol = cols[:, 18:20]
        hre, him = cols[:, 18:19], cols[:, 19:20]
        a_r, a_i = cols[:, 11:12], cols[:, 12:13]
        t_s = cols[:, 13:14]
        toks = []
        t0 = E("V", lambda e: e.tensor_copy(out=hre, in_=sprevR[:, 0:1]), deps=(c5,) + sdone)
        t0b = E("V", lambda e: e.tensor_copy(out=him, in_=sprevI[:, 0:1]), deps=(c5, t0))
        prevh = (t0, t0b)
        for j, mult in ((1, 1024.0), (2, 2048.0)):
            m1 = E("A", lambda e, mult=mult: e.activation(out=a_r, in_=ccols["lr"], func=AF.Exp,
                                                          scale=mult), deps=sdone + prevh)
            p1 = E("V", lambda e, mult=mult: e.tensor_scalar(out=t_s, in0=ccols["li"], scalar1=mult,
                                                             scalar2=None, op0=ALU.mult),
                   deps=sdone + prevh)
            sn2, cs2 = emit_sincos(E, t_s, cols[:, 14:15], cols[:, 15:16], cols[:, 16:17],
                                   cols[:, 17:18], coli[:, 0:1], (p1,))
            m2 = E("V", lambda e: e.tensor_tensor(out=a_i, in0=a_r, in1=cols[:, 14:15],
                                                  op=ALU.mult), deps=(m1, sn2))
            m3 = E("V", lambda e: e.tensor_tensor(out=a_r, in0=a_r, in1=cols[:, 15:16],
                                                  op=ALU.mult), deps=(m1, cs2, m2))
            q1 = E("V", lambda e, j=j: e.tensor_tensor(out=cols[:, 14:15], in0=a_r,
                                                       in1=sprevR[:, j:j + 1], op=ALU.mult),
                   deps=(m3,))
            q2 = E("V", lambda e, j=j: e.tensor_tensor(out=cols[:, 15:16], in0=a_i,
                                                       in1=sprevI[:, j:j + 1], op=ALU.mult),
                   deps=(m2, q1))
            q3 = E("V", lambda e: e.tensor_tensor(out=cols[:, 14:15], in0=cols[:, 14:15],
                                                  in1=cols[:, 15:16], op=ALU.subtract),
                   deps=(q1, q2))
            q4 = E("V", lambda e: e.tensor_tensor(out=hre, in0=hre, in1=cols[:, 14:15],
                                                  op=ALU.add), deps=(q3,) + prevh)
            q5 = E("V", lambda e, j=j: e.tensor_tensor(out=cols[:, 14:15], in0=a_r,
                                                       in1=sprevI[:, j:j + 1], op=ALU.mult),
                   deps=(q4, m3))
            q6 = E("V", lambda e, j=j: e.tensor_tensor(out=cols[:, 15:16], in0=a_i,
                                                       in1=sprevR[:, j:j + 1], op=ALU.mult),
                   deps=(q3, m2))
            q7 = E("V", lambda e: e.tensor_tensor(out=cols[:, 14:15], in0=cols[:, 14:15],
                                                  in1=cols[:, 15:16], op=ALU.add), deps=(q5, q6))
            q8 = E("V", lambda e: e.tensor_tensor(out=him, in0=him, in1=cols[:, 14:15],
                                                  op=ALU.add), deps=(q7,) + prevh)
            prevh = (q4, q8)
        z1 = E("V", lambda e: e.tensor_scalar(out=SN, in0=T2, scalar1=hre, scalar2=None,
                                              op0=ALU.mult), deps=prevh + (f2,))
        z2 = E("V", lambda e: e.tensor_scalar(out=T1, in0=CS, scalar1=him, scalar2=None,
                                              op0=ALU.mult), deps=prevh + (f2, em))
        z3 = E("V", lambda e: e.tensor_tensor(out=SN, in0=SN, in1=T1, op=ALU.subtract),
               deps=(z1, z2))
        z4 = E("V", lambda e: e.tensor_tensor(out=XL_re[:], in0=XL_re[:], in1=SN,
                                              op=ALU.add), deps=(z3, c6))
        z5 = E("V", lambda e: e.tensor_scalar(out=SN, in0=CS, scalar1=hre, scalar2=None,
                                              op0=ALU.mult), deps=(z4, z3))
        z6 = E("V", lambda e: e.tensor_scalar(out=T1, in0=T2, scalar1=him, scalar2=None,
                                              op0=ALU.mult), deps=(z2, z5))
        z7 = E("V", lambda e: e.tensor_tensor(out=SN, in0=SN, in1=T1, op=ALU.add), deps=(z5, z6))
        z8 = E("V", lambda e: e.tensor_tensor(out=XL_im[:], in0=XL_im[:], in1=SN,
                                              op=ALU.add), deps=(z7, c6))
        xb_a = E("V", lambda e: e.tensor_copy(out=Xb[0:32, :], in_=XL_re[:]), deps=(z4, z8))
        xb = E("V", lambda e: e.tensor_copy(out=Xb[32:64, :], in_=XL_im[:]), deps=(z8, xb_a))

        out_dmas = []
        xst_users = {}
        prev_pp = {0: (xb,), 1: (xb,)}
        for mt in range(DCH):
            par = mt % 2
            pD = (pA, pB)[par]
            mm = []
            for h in range(2):
                mm.append(E("T", lambda e, mt=mt, h=h, pD=pD: e.matmul(
                    pD[:, h * H:(h + 1) * H], ctb[:, mt * 128:(mt + 1) * 128],
                    Xb[:, h * H:(h + 1) * H], start=True, stop=True),
                    deps=(xb, ng, c8) + prev_pp[par]))
            for h in range(2):
                b = h
                deps = xst_users.pop(b, ())
                ldx = E(f"dS:x{b}", lambda e, mt=mt, h=h, b=b: e.dma_start(
                    out=xst[:, b * H:(b + 1) * H], in_=x[mt, :, h * H:(h + 1) * H]), deps=deps)
                ut = E("V", lambda e, mt=mt, h=h, b=b: e.tensor_scalar(
                    out=utmp[:, b * H:(b + 1) * H],
                    in0=usb[:, mt * TC + h * H: mt * TC + (h + 1) * H],
                    scalar1=dcols[:, mt:mt + 1], scalar2=None, op0=ALU.mult),
                    deps=(c9, c7) + deps)
                ev = E("V", lambda e, h=h, b=b, pD=pD: e.scalar_tensor_tensor(
                    out=xst[:, b * H:(b + 1) * H], in0=pD[:, h * H:(h + 1) * H], scalar=2.0,
                    in1=xst[:, b * H:(b + 1) * H], op0=ALU.mult, op1=ALU.add),
                    deps=(ldx, mm[h]))
                ev2 = E("V", lambda e, h=h, b=b: e.tensor_tensor(
                    out=xst[:, b * H:(b + 1) * H], in0=xst[:, b * H:(b + 1) * H],
                    in1=utmp[:, b * H:(b + 1) * H], op=ALU.add), deps=(ev, ut))
                st = E(f"dS:o{b}", lambda e, mt=mt, h=h, b=b: e.dma_start(
                    out=xo[mt, :, h * H:(h + 1) * H], in_=xst[:, b * H:(b + 1) * H]), deps=(ev2,))
                out_dmas.append(st)
                xst_users[b] = (st, ev2)
            prev_pp[par] = (out_dmas[-1], out_dmas[-2])
        p.finish(out_dmas)
        p.build(es)
    return nc


def build_l1a():
    nc = bass.Bass("TRN2", target_bir_lowering=False)

    def din(name, shape, dt=F32):
        return nc.dram_tensor(name, shape, dt, kind="ExternalInput").ap()

    x = din("x", [DCH, 128, TC])
    winzB = din("winzB", [8, DCH, 128, 512])
    winxB = din("winxB", [32, DCH, 128, 128])
    wins = din("wins", [DCH, 128, 288])
    alog_d = din("alog", [64, 1])
    dtb_d = din("dtb", [64, 1])
    bnb_d = din("bnb", [64, 1])
    cnb_d = din("cnb", [64, 1])
    onesc_d = din("onesc", [128, 1])
    onesf_d = din("onesf", [1, 128])
    o_xh = nc.dram_tensor("o_xh", [32, 128, TC], BF16, kind="ExternalOutput").ap()
    o_sz = nc.dram_tensor("o_sz", [8, 128, 4096], BF16, kind="ExternalOutput").ap()
    o_sm = nc.dram_tensor("o_sm", [7, 64, TC], F32, kind="ExternalOutput").ap()

    p = Prog(nc)
    es = ExitStack()
    with es:
        sb = lambda n, shape, dt: es.enter_context(nc.sbuf_tensor(n, shape, dt))
        u = sb("u", [128, DCH * TC], BF16)
        xst = sb("xst", [128, TC], F32)
        rows2 = sb("rows2", [1, 2 * TC], F32)
        wz = sb("wz", [128, 2 * DCH * 512], BF16)
        wx = sb("wx", [128, 2 * DCH * 128], BF16)
        ws = sb("ws", [128, 2 * 288], BF16)
        zstage = sb("zstage", [128, 4096], BF16)
        xstage = sb("xstage", [128, 2 * TC], BF16)
        sm = sb("sm", [128, 11 * TC], F32)
        cols64 = sb("cols64", [64, 8], F32)
        q64 = sb("q64", [64, TC], BF16)
        rowq = sb("rowq", [1, TC], F32)
        onesc = sb("s_onesc", [128, 1], F32)
        onesc_b = sb("s_onesc_b", [128, 1], BF16)
        onesf = sb("s_onesf", [1, 128], F32)
        pA = es.enter_context(nc.psum_tensor("pA", [128, TC], F32))
        pB = es.enter_context(nc.psum_tensor("pB", [128, TC], F32))
        pE = es.enter_context(nc.psum_tensor("pE", [128, TC], F32))
        pF = es.enter_context(nc.psum_tensor("pF", [128, TC], F32))

        E = p.emit
        H = 512
        c0 = E("dA:c", lambda e: e.dma_start(out=onesc[:], in_=onesc_d))
        c1 = E("dP:c", lambda e: e.dma_start(out=onesc_b[:], in_=onesc_d))
        c2 = E("dA:c", lambda e: e.dma_start(out=onesf[:], in_=onesf_d), deps=(c0,))
        c3 = E("dA:c", lambda e: e.dma_start(out=cols64[:, 0:1], in_=alog_d), deps=(c2,))
        c4 = E("dA:c", lambda e: e.dma_start(out=cols64[:, 1:2], in_=dtb_d), deps=(c3,))
        c5 = E("dA:c", lambda e: e.dma_start(out=cols64[:, 2:3], in_=bnb_d), deps=(c4,))
        c6 = E("dA:c", lambda e: e.dma_start(out=cols64[:, 3:4], in_=cnb_d), deps=(c5,))

        u_done, xst_users, _ = emit_ln(p, E, x, u, pA, pB, pE, xst, rows2, onesc, onesc_b,
                                       onesf, dep0=(c0, c1))
        UD = tuple(u_done.values())

        wz_last = {}
        prevjs = ()
        for js in range(8):
            sl = js % 2
            ldz = E(f"dP:wz{sl}", lambda e, js=js, sl=sl: e.dma_start(
                out=wz[:, sl * DCH * 512:(sl + 1) * DCH * 512].rearrange("p (c f) -> p c f", c=DCH),
                in_=winzB[js].rearrange("c p f -> p c f")),
                deps=wz_last.pop(sl, ()))
            use = []
            for tt in range(8):
                mm = []
                for kc in range(DCH):
                    mm.append(E("T", lambda e, js=js, sl=sl, tt=tt, kc=kc: e.matmul(
                        pF[:, (tt % 2) * H:(tt % 2 + 1) * H],
                        u[:, kc * TC + tt * 128: kc * TC + (tt + 1) * 128],
                        wz[:, sl * DCH * 512 + kc * 512: sl * DCH * 512 + (kc + 1) * 512],
                        start=(kc == 0), stop=(kc == DCH - 1)),
                        deps=(ldz, u_done[(kc, tt // 4)])
                        + ((use[-2],) if len(use) >= 2 else (prevjs if kc == 0 else ()))))
                ev = E("A", lambda e, js=js, tt=tt: e.activation(
                    out=zstage[:, js * 512:(js + 1) * 512], in_=pF[:, (tt % 2) * H:(tt % 2 + 1) * H],
                    func=AF.Silu), deps=(mm[-1],) + ((use[-1],) if use else ()))
                st = E(f"dS:z{tt % 2}", lambda e, js=js, tt=tt: e.dma_start(
                    out=o_sz[tt, :, js * 512:(js + 1) * 512], in_=zstage[:, js * 512:(js + 1) * 512]),
                    deps=(ev,))
                use.append(st)
            wz_last[sl] = tuple(use)
            prevjs = (use[-1], use[-2])

        wx_last = {}
        for jt in range(32):
            sl = jt % 2
            ldx = E(f"dP:wx{sl}", lambda e, jt=jt, sl=sl: e.dma_start(
                out=wx[:, sl * DCH * 128:(sl + 1) * DCH * 128].rearrange("p (c f) -> p c f", c=DCH),
                in_=winxB[jt].rearrange("c p f -> p c f")),
                deps=wx_last.pop(sl, ()))
            mm = []
            for kc in range(DCH):
                for h in range(2):
                    mm.append(E("T", lambda e, jt=jt, sl=sl, kc=kc, h=h: e.matmul(
                        pA[:, h * H:(h + 1) * H] if jt % 2 == 0 else pB[:, h * H:(h + 1) * H],
                        wx[:, sl * DCH * 128 + kc * 128: sl * DCH * 128 + (kc + 1) * 128],
                        u[:, kc * TC + h * H: kc * TC + (h + 1) * H],
                        start=(kc == 0), stop=(kc == DCH - 1)),
                        deps=(ldx, u_done[(kc, h)]) + (wx_last.get("ev" + str(jt % 2), ()))))
            ev = E("A", lambda e, jt=jt: e.activation(
                out=xstage[:, (jt % 2) * TC:(jt % 2 + 1) * TC],
                in_=pA[:] if jt % 2 == 0 else pB[:], func=AF.Silu),
                deps=(mm[-1],) + wx_last.get("st" + str(jt % 2), ()))
            st = E(f"dS:xh{jt % 2}", lambda e, jt=jt: e.dma_start(
                out=o_xh[jt], in_=xstage[:, (jt % 2) * TC:(jt % 2 + 1) * TC]), deps=(ev,))
            wx_last["ev" + str(jt % 2)] = (ev,)
            wx_last["st" + str(jt % 2)] = (st,)
            wx_last[sl] = tuple(mm)
        xh_done = tuple(wx_last["st0"] + wx_last["st1"]) if False else (wx_last["st0"] + wx_last["st1"])

        sm_mm = {0: [], 1: [], 2: []}
        ws_last = {}
        for kc in range(DCH):
            sl = kc % 2
            ld = E(f"dP:ws{sl}", lambda e, kc=kc, sl=sl: e.dma_start(
                out=ws[:, sl * 288:(sl + 1) * 288], in_=wins[kc]), deps=ws_last.pop(sl, ()))
            kcmm = []
            for mt, (mof, mw) in enumerate(((0, 128), (128, 128), (256, 32))):
                for h in range(2):
                    kcmm.append(E("T", lambda e, kc=kc, sl=sl, mt=mt, mof=mof, mw=mw, h=h: e.matmul(
                        (pE, pF, pA)[mt][0:mw, h * H:(h + 1) * H],
                        ws[:, sl * 288 + mof: sl * 288 + mof + mw],
                        u[:, kc * TC + h * H: kc * TC + (h + 1) * H],
                        start=(kc == 0), stop=(kc == DCH - 1)),
                        deps=(ld, u_done[(kc, h)]) +
                        ((xh_done + tuple(wz_last[0] + wz_last[1])) if kc == 0 else ())))
                    sm_mm[mt].append(kcmm[-1])
            ws_last[sl] = tuple(kcmm)
        SM = {k: sm[:, k * TC:(k + 1) * TC] for k in range(8)}
        evbc = E("V", lambda e: e.tensor_copy(out=sm[:, 0:TC], in_=pE[:, 0:TC]),
                 deps=(sm_mm[0][-1],))
        evdl = E("V", lambda e: e.tensor_copy(out=sm[:, TC:2 * TC], in_=pF[:, 0:TC]),
                 deps=(sm_mm[1][-1],))
        evth = E("V", lambda e: e.tensor_copy(out=sm[:, 2 * TC:2 * TC + TC][0:32, :] if False else sm[0:32, 2 * TC:3 * TC], in_=pA[0:32, 0:TC]),
                 deps=(sm_mm[2][-1],))
        bcs = sm[:, 0:TC]
        dl = sm[:, TC:2 * TC]
        thr = sm[0:32, 2 * TC:3 * TC]

        outs = []
        for half, (bias_col, extra, oidx) in enumerate((("b", (), 0), ("c", (), 1))):
            base = half * 64
            sq = E("A", lambda e, base=base: e.activation(out=q64[:], in_=bcs[base:base + 64, :] if base == 0 else bcs[64:128, :],
                                                          func=AF.Square), deps=(evbc,) + tuple(outs))
            mmq = E("T", lambda e: e.matmul(pB[0:1, 0:H], onesc_b[0:64, :], q64[:, 0:H],
                                            start=True, stop=True), deps=(sq, c1) + tuple(outs))
            mmq2 = E("T", lambda e: e.matmul(pB[0:1, H:TC], onesc_b[0:64, :], q64[:, H:TC],
                                             start=True, stop=True), deps=(sq, c1))
            r1 = E("V", lambda e: e.tensor_scalar(out=rowq[:], in0=pB[0:1, 0:TC], scalar1=1.0 / 64,
                                                  scalar2=1e-6, op0=ALU.mult, op1=ALU.add),
                   deps=(mmq, mmq2))
            r2 = E("A", lambda e: e.activation(out=rowq[:], in_=rowq[:], func=AF.Ln), deps=(r1,))
            r3 = E("A", lambda e: e.activation(out=rowq[:], in_=rowq[:], func=AF.Exp, scale=-0.5),
                   deps=(r2,))
            b1 = E("T", lambda e: e.matmul(pB[0:64, 0:H], onesf[0:1, 0:64], rowq[0:1, 0:H],
                                           start=True, stop=True), deps=(r3,))
            b2 = E("T", lambda e: e.matmul(pB[0:64, H:TC], onesf[0:1, 0:64], rowq[0:1, H:TC],
                                           start=True, stop=True), deps=(r3,))
            sc1 = 1.0 if half == 0 else 0.125
            bc_ap = cols64[:, 2:3] if half == 0 else cols64[:, 3:4]
            t1 = E("V", lambda e, base=base: e.tensor_tensor(
                out=q64[:], in0=bcs[base:base + 64, :] if base == 0 else bcs[64:128, :],
                in1=pB[0:64, 0:TC], op=ALU.mult), deps=(b1, b2, sq))
            dst = sm[0:64, (3 + oidx) * TC:(4 + oidx) * TC] if False else None
            t2 = E("V", lambda e, half=half, bc_ap=bc_ap, sc1=sc1: e.tensor_scalar(
                out=sm[0:64, (6 + half) * TC:(7 + half) * TC], in0=q64[:], scalar1=bc_ap,
                scalar2=sc1, op0=ALU.mult, op1=ALU.mult), deps=(t1, c5, c6))
            outs.append(t2)
        bn_t, cn_t = outs
        bn_ap = sm[0:64, 6 * TC:7 * TC]
        cn_ap = sm[0:64, 7 * TC:8 * TC]

        dtr = dl[0:64, :] if False else sm[0:64, TC:2 * TC]
        lamr = sm[64:128, TC:2 * TC]
        g1 = E("V", lambda e: e.tensor_scalar(out=sm[0:64, 3 * TC:4 * TC], in0=dtr,
                                              scalar1=cols64[:, 1:2], scalar2=None, op0=ALU.add),
               deps=(evdl, c4))
        DT = sm[0:64, 3 * TC:4 * TC]
        g2 = E("A", lambda e: e.activation(out=sm[0:64, 4 * TC:5 * TC], in_=DT, func=AF.Abs),
               deps=(g1,))
        g3 = E("A", lambda e: e.activation(out=sm[0:64, 4 * TC:5 * TC], in_=sm[0:64, 4 * TC:5 * TC],
                                           func=AF.Exp, scale=-1.0), deps=(g2,))
        g4 = E("A", lambda e: e.activation(out=sm[0:64, 4 * TC:5 * TC], in_=sm[0:64, 4 * TC:5 * TC],
                                           func=AF.Ln, bias=onesc[0:64, 0:1]), deps=(g3,))
        g5 = E("V", lambda e: e.tensor_scalar(out=sm[0:64, 5 * TC:6 * TC], in0=DT, scalar1=0.0,
                                              scalar2=None, op0=ALU.max), deps=(g1,))
        g6 = E("V", lambda e: e.tensor_tensor(out=DT, in0=sm[0:64, 4 * TC:5 * TC],
                                              in1=sm[0:64, 5 * TC:6 * TC], op=ALU.add),
               deps=(g4, g5))
        g7 = E("V", lambda e: e.tensor_scalar(out=DT, in0=DT, scalar1=0.5, scalar2=1e-4,
                                              op0=ALU.min, op1=ALU.max), deps=(g6,))
        a1_ = E("A", lambda e: e.activation(out=cols64[:, 4:5], in_=cols64[:, 0:1], func=AF.Exp),
                deps=(c3,))
        a2_ = E("V", lambda e: e.tensor_scalar(out=cols64[:, 4:5], in0=cols64[:, 4:5], scalar1=-1.0,
                                               scalar2=None, op0=ALU.mult), deps=(a1_,))
        DA = sm[0:64, 4 * TC:5 * TC]
        a3_ = E("V", lambda e: e.tensor_scalar(out=DA, in0=DT, scalar1=cols64[:, 4:5],
                                               scalar2=None, op0=ALU.mult), deps=(g7, a2_, g6, g4))
        AL = sm[0:64, 5 * TC:6 * TC]
        a4_ = E("A", lambda e: e.activation(out=AL, in_=DA, func=AF.Exp), deps=(a3_, g6, g5))
        LM = q64[:]
        a5_ = E("A", lambda e: e.activation(out=sm[0:64, 8 * TC:9 * TC], in_=lamr,
                                            func=AF.Sigmoid), deps=(evdl, g1))
        LAM = sm[0:64, 8 * TC:9 * TC]
        w1_ = E("V", lambda e: e.tensor_tensor(out=sm[0:64, 9 * TC:10 * TC], in0=LAM, in1=DT,
                                               op=ALU.mult), deps=(a5_, g7))
        w2a = E("V", lambda e: e.tensor_scalar(out=sm[0:64, 10 * TC:11 * TC], in0=LAM, scalar1=-1.0,
                                               scalar2=1.0, op0=ALU.mult, op1=ALU.add),
                deps=(a5_,))
        w2b = E("V", lambda e: e.tensor_tensor(out=sm[0:64, 10 * TC:11 * TC],
                                               in0=sm[0:64, 10 * TC:11 * TC], in1=AL, op=ALU.mult),
                deps=(w2a, a4_))
        w2c = E("V", lambda e: e.tensor_tensor(out=sm[0:64, 10 * TC:11 * TC],
                                               in0=sm[0:64, 10 * TC:11 * TC], in1=DT, op=ALU.mult),
                deps=(w2b, g7))
        m1 = E("T", lambda e: e.matmul(pB[0:1, 0:H], onesc[0:64, :], DT[:, 0:H], start=True,
                                       stop=True), deps=(g7, a3_, w1_, w2c))
        m2 = E("T", lambda e: e.matmul(pB[0:1, H:TC], onesc[0:64, :], DT[:, H:TC], start=True,
                                       stop=True), deps=(g7, a3_, w1_, w2c))
        m3 = E("V", lambda e: e.tensor_scalar(out=rowq[:], in0=pB[0:1, 0:TC], scalar1=1.0 / 64,
                                              scalar2=None, op0=ALU.mult), deps=(m1, m2, b1, b2, r3))
        m4 = E("T", lambda e: e.matmul(pB[0:32, 0:H], onesf[0:1, 0:32], rowq[0:1, 0:H],
                                       start=True, stop=True), deps=(m3,))
        m5 = E("T", lambda e: e.matmul(pB[0:32, H:TC], onesf[0:1, 0:32], rowq[0:1, H:TC],
                                       start=True, stop=True), deps=(m3,))
        m6 = E("V", lambda e: e.tensor_tensor(out=thr, in0=thr, in1=pB[0:32, 0:TC], op=ALU.mult),
               deps=(m4, m5, evth))

        od = []
        od.append(E("dS:o0", lambda e: e.dma_start(out=o_sm[0, 0:64], in_=bn_ap), deps=(bn_t,)))
        od.append(E("dS:o0", lambda e: e.dma_start(out=o_sm[1, 0:64], in_=cn_ap), deps=(cn_t, od[-1])))
        od.append(E("dS:o0", lambda e: e.dma_start(out=o_sm[2, 0:64], in_=DT), deps=(g7, od[-1])))
        od.append(E("dS:o0", lambda e: e.dma_start(out=o_sm[3, 0:64], in_=DA), deps=(a3_, a4_, od[-1])))
        od.append(E("dS:o0", lambda e: e.dma_start(out=o_sm[4, 0:64], in_=sm[0:64, 9 * TC:10 * TC]),
                   deps=(w1_, od[-1])))
        od.append(E("dS:o0", lambda e: e.dma_start(out=o_sm[5, 0:64], in_=sm[0:64, 10 * TC:11 * TC]),
                   deps=(w2c, od[-1])))
        od.append(E("dS:o0", lambda e: e.dma_start(out=o_sm[6, 0:32], in_=thr), deps=(m6, od[-1])))
        p.finish(tuple(od) + xh_done + tuple(wz_last[0] + wz_last[1]))
        p.build(es)
    return nc


NH = 16
TL = 4096
NCH = 16


def build_l1b():
    nc = bass.Bass("TRN2", target_bir_lowering=False)

    def din(name, shape, dt=F32):
        return nc.dram_tensor(name, shape, dt, kind="ExternalInput").ap()

    xh_d = din("xh", [8, 128, TL], BF16)
    bn_d = din("bn", [64, TL])
    cn_d = din("cn", [64, TL])
    thw_d = din("thw", [32, TL])
    da_d = din("da16", [NH, TL])
    w1t_d = din("w1t16", [NH, TL])
    w2t_d = din("w2t16", [NH, TL])
    triu_d = din("triu", [2, 128, 256])
    rst_d = din("rst256", [NH, TL])
    identf_d = din("identf", [128, 128])
    identb_d = din("identb", [128, 128])
    onesf_d = din("onesf", [1, 128])
    y_o = nc.dram_tensor("y", [32, 128, 1024], BF16, kind="ExternalOutput").ap()
    csf_d = nc.dram_tensor("csflat", [NH, 256], F32, kind="Internal").ap()

    p = Prog(nc)
    es = ExitStack()
    with es:
        sb = lambda n, shape, dt: es.enter_context(nc.sbuf_tensor(n, shape, dt))
        AR = sb("AR", [128, 21760], F32)
        btb = sb("btb", [64, TL], BF16)
        ctb = sb("ctb", [64, TL], BF16)
        bshb = sb("bshb", [64, TL], BF16)
        cs16_t = sb("cs16", [48, TL], F32)
        cs16 = cs16_t[32:48, :]
        cstm = sb("cstm", [128, NCH * 2 * NH], F32)
        eptm = sb("eptm", [128, NCH * 2 * NH], F32)
        wdtm = sb("wdtm", [128, NCH * 2 * NH], F32)
        w1tm = sb("w1tm", [128, NCH * 2 * NH], F32)
        w2tm = sb("w2tm", [128, NCH * 2 * NH], F32)
        decrow = sb("decrow", [1, NCH * NH], F32)
        ints = sb("ints", [32, 1024], I32)
        XHC = sb("XHC", [128, 2 * 8 * 258], BF16)
        xd1 = sb("xd1", [128, 2048], BF16)
        xd2 = sb("xd2", [128, 2048], BF16)
        xw1 = sb("xw1", [128, 2048], BF16)
        xw2 = sb("xw2", [128, 2048], BF16)
        btm = sb("btm", [128, 128], BF16)
        bshm = sb("bshm", [128, 128], BF16)
        g1b = sb("g1b", [128, 512], BF16)
        g2b = sb("g2b", [128, 512], BF16)
        fb = sb("fb", [128, 1024], BF16)
        m1b = sb("m1b", [128, 1024], BF16)
        m2b = sb("m2b", [128, 1024], BF16)
        dsc = sb("dsc", [128, 1024], F32)
        ytile = sb("ytile", [128, 2048], BF16)
        ysc = sb("ysc", [128, 1024], F32)
        Hst = sb("Hst", [64, 1024], F32)
        Hsc = sb("Hsc", [64, 1024], F32)
        Hb = sb("Hb", [64, 1024], BF16)
        identf = sb("s_identf", [128, 128], F32)
        identb = sb("s_identb", [128, 128], BF16)
        onesf = sb("s_onesf", [1, 128], F32)
        triu = sb("s_triu", [128, 2 * 256], F32)
        pA2 = es.enter_context(nc.psum_tensor("pA2", [128, 1024], F32))
        pB2 = es.enter_context(nc.psum_tensor("pB2", [128, 1024], F32))
        pY = es.enter_context(nc.psum_tensor("pY", [128, 1024], F32))
        pF = es.enter_context(nc.psum_tensor("pF", [128, 512], F32))
        pT = es.enter_context(nc.psum_tensor("pT", [128, 512], BF16))

        E = p.emit
        c0 = E("dA:c", lambda e: e.dma_start(out=onesf[:], in_=onesf_d))
        c1 = E("dA:c", lambda e: e.dma_start(out=identf[:], in_=identf_d), deps=(c0,))
        c2 = E("dP:c", lambda e: e.dma_start(out=identb[:], in_=identb_d))
        c3 = E("dA:c", lambda e: e.dma_start(
            out=triu[:].rearrange("p (a f) -> p a f", a=2), in_=triu_d.rearrange("a p f -> p a f")),
            deps=(c1,))
        A_BN1 = AR[0:32, 0:TL]
        A_BN2 = AR[0:32, TL:2 * TL]
        A_CN1 = AR[0:32, 2 * TL:3 * TL]
        A_CN2 = AR[0:32, 3 * TL:4 * TL]
        A_THW = AR[32:64, 0:TL]
        A_CUM = AR[32:64, 2 * TL:3 * TL]
        l_bn1 = E("dS:ld", lambda e: e.dma_start(out=A_BN1, in_=bn_d[0:32]))
        l_bn = E("dS:ld", lambda e: e.dma_start(out=A_BN2, in_=bn_d[32:64]), deps=(l_bn1,))
        l_cn1 = E("dS:ld", lambda e: e.dma_start(out=A_CN1, in_=cn_d[0:32]), deps=(l_bn,))
        l_cn = E("dS:ld", lambda e: e.dma_start(out=A_CN2, in_=cn_d[32:64]), deps=(l_cn1,))
        l_th = E("dS:ld", lambda e: e.dma_start(out=A_THW, in_=thw_d), deps=(l_cn,))
        l_da = E("dS:ld", lambda e: e.dma_start(out=cs16, in_=da_d), deps=(l_th,))
        l_w1 = E("dS:ld", lambda e: e.dma_start(out=AR[0:NH, 16384:16384 + TL], in_=w1t_d),
                 deps=(l_da,))
        l_w2 = E("dS:ld", lambda e: e.dma_start(out=AR[32:48, 16384:16384 + TL], in_=w2t_d),
                 deps=(l_w1,))
        l_rs = E("dS:ld", lambda e: e.dma_start(out=AR[32:48, 3 * TL:4 * TL], in_=rst_d),
                 deps=(l_w2,))
        W1P = AR[0:NH, 16384:16384 + TL]
        W2P = AR[32:48, 16384:16384 + TL]
        RSP = AR[32:48, 3 * TL:4 * TL]

        t_cs = E("V", lambda e: e.tensor_tensor_scan(out=cs16, data0=RSP, data1=cs16,
                                                     initial=0.0, op0=ALU.mult, op1=ALU.add),
                 deps=(l_da, l_rs))
        up_tok = []
        dec_tok = []
        for c in range(NCH):
            for st in range(2):
                col = c * 256 + st * 128
                dst = c * 2 * NH + st * NH
                tw1 = E("T", lambda e, col=col: e.transpose(
                    pF[:, 0:NH], W1P[:, col:col + 128], identf[0:NH, 0:NH]), deps=(l_w1, c1))
                ew1 = E("V", lambda e, dst=dst: e.tensor_copy(out=w1tm[:, dst:dst + NH],
                                                              in_=pF[:, 0:NH]), deps=(tw1,))
                tw2 = E("T", lambda e, col=col: e.transpose(
                    pF[:, NH:2 * NH], W2P[:, col:col + 128], identf[32:32 + NH, 32:32 + NH]),
                    deps=(l_w2, c1, ew1))
                ew2 = E("V", lambda e, dst=dst: e.tensor_copy(out=w2tm[:, dst:dst + NH],
                                                              in_=pF[:, NH:2 * NH]), deps=(tw2,))
                tcs = E("T", lambda e, col=col: e.transpose(
                    pF[:, 2 * NH:3 * NH], cs16[:, col:col + 128], identf[32:32 + NH, 32:32 + NH]),
                    deps=(t_cs, c1, ew2))
                ecs = E("V", lambda e, dst=dst: e.tensor_copy(out=cstm[:, dst:dst + NH],
                                                              in_=pF[:, 2 * NH:3 * NH]), deps=(tcs,))
                eep = E("A", lambda e, dst=dst: e.activation(out=eptm[:, dst:dst + NH],
                                                             in_=cstm[:, dst:dst + NH],
                                                             func=AF.Exp), deps=(ecs,))
                wfm = E("A", lambda e, c=c, col=col: e.activation(
                    out=AR[0:16, 20480:20608],
                    in_=cs16[:, col:col + 128], func=AF.Exp, scale=-1.0,
                    bias=cs16[:, c * 256 + 255:c * 256 + 256]), deps=(t_cs, ecs))
                twd = E("T", lambda e: e.transpose(
                    pF[:, 3 * NH:4 * NH], AR[0:16, 20480:20608], identf[0:NH, 0:NH]),
                    deps=(wfm,))
                ewd = E("V", lambda e, dst=dst: e.tensor_copy(out=wdtm[:, dst:dst + NH],
                                                              in_=pF[:, 3 * NH:4 * NH]), deps=(twd,))
                up_tok.extend((ew1, ew2, ecs, eep, ewd))
                if st == 0:
                    dc1 = E("A", lambda e, c=c: e.activation(
                        out=AR[0:16, 20608:20609], in_=cs16[:, c * 256 + 255:c * 256 + 256],
                        func=AF.Exp), deps=(t_cs, twd, ewd))
                    dc2 = E("T", lambda e: e.transpose(
                        pF[0:1, 4 * NH:5 * NH], AR[0:16, 20608:20609], identf[0:NH, 0:NH]),
                        deps=(dc1,))
                    dc3 = E("V", lambda e, c=c: e.tensor_copy(out=decrow[0:1, c * NH:(c + 1) * NH],
                                                              in_=pF[0:1, 4 * NH:4 * NH + 1].rearrange("p a -> p a") if False else pF[0:1, 4 * NH:5 * NH]),
                            deps=(dc2,))
                    dec_tok.append(dc3)
        t_cum = E("V", lambda e: e.tensor_tensor_scan(
            out=A_CUM, data0=AR[32:64, 2 * TL:3 * TL][0:32, :] if False else A_CUM, data1=A_THW,
            initial=0.0, op0=ALU.mult, op1=ALU.add), deps=(l_th,))
        p.ops.pop()
        ONES32 = AR[32:64, TL:2 * TL]
        msk1 = E("V", lambda e: e.memset(ONES32, 1.0), deps=(l_th,))
        t_cum = E("V", lambda e: e.tensor_tensor_scan(
            out=A_CUM, data0=ONES32, data1=A_THW, initial=0.0, op0=ALU.mult, op1=ALU.add),
            deps=(l_th, msk1))
        rope_done = []
        last_q = ()
        for q in range(4):
            QS = 1024
            sl = slice(q * QS, (q + 1) * QS)
            S1 = AR[0:32, 16384 + 0 * QS:16384 + 1 * QS]
            S2 = AR[0:32, 16384 + 1 * QS:16384 + 2 * QS]
            SNq = AR[0:32, 16384 + 2 * QS:16384 + 3 * QS]
            CSq = AR[0:32, 16384 + 3 * QS:16384 + 4 * QS]
            a0 = E("V", lambda e, sl=sl: e.tensor_scalar(out=S1, in0=A_CUM[:, sl], scalar1=-1.0,
                                                         scalar2=None, op0=ALU.mult),
                   deps=(t_cum,) + last_q + ((up_tok[-1], l_w1, l_w2, l_rs) if q == 0 else ()))
            S2b = AR[0:32, 20736:21760]
            snt, cst = emit_sincos(E, S1, SNq, CSq, S2, S2b, ints[:, 0:1024], (a0,))
            rope_q = []
            for dst, (src1, src2), tok0 in ((btb, (A_BN1, A_BN2), l_bn), (ctb, (A_CN1, A_CN2), l_cn)):
                b1h = src1[0:32, sl]
                b2h = src2[0:32, sl]
                r1a = E("V", lambda e, b1h=b1h: e.tensor_tensor(out=S1, in0=b1h, in1=CSq,
                                                                op=ALU.mult),
                        deps=(cst, tok0, snt) + tuple(rope_q[-1:]))
                r1b = E("V", lambda e, b2h=b2h: e.tensor_tensor(out=S2, in0=b2h, in1=SNq,
                                                                op=ALU.mult), deps=(snt, tok0, r1a))
                r1c = E("V", lambda e: e.tensor_tensor(out=S1, in0=S1, in1=S2, op=ALU.subtract),
                        deps=(r1a, r1b))
                w1b = E("V", lambda e, dst=dst, sl=sl: e.tensor_copy(out=dst[0:32, sl], in_=S1),
                        deps=(r1c,))
                r2a = E("V", lambda e, b1h=b1h: e.tensor_tensor(out=S1, in0=b1h, in1=SNq,
                                                                op=ALU.mult), deps=(w1b, snt))
                r2b = E("V", lambda e, b2h=b2h: e.tensor_tensor(out=S2, in0=b2h, in1=CSq,
                                                                op=ALU.mult), deps=(r1b, cst, r2a))
                r2c = E("V", lambda e: e.tensor_tensor(out=S1, in0=S1, in1=S2, op=ALU.add),
                        deps=(r2a, r2b))
                w2b = E("V", lambda e, dst=dst, sl=sl: e.tensor_copy(out=dst[32:64, sl], in_=S1),
                        deps=(r2c,))
                rope_q.extend((w1b, w2b))
            last_q = tuple(rope_q)
            rope_done.extend(rope_q)
        sh1 = E("V", lambda e: e.memset(bshb[:, 0:1], 0.0), deps=tuple(rope_done))
        sh2 = E("V", lambda e: e.tensor_copy(out=bshb[:, 1:TL], in_=btb[:, 0:TL - 1]),
                deps=tuple(rope_done) + (sh1,))

        ydmas = []
        H_prev = None
        hz = E("V", lambda e: e.memset(Hst[:], 0.0))
        hzb = E("V", lambda e: e.memset(Hb[:], 0.0), deps=(hz,))
        H_prev = (hz, hzb)
        xhc_users = {}
        cs_dma_prev = ()
        prev_evtoks = ()
        prev_mask_last = ()
        d1_toks = []
        last_d2 = {}
        last_d34 = {}
        for c in range(NCH):
            sl = c % 2
            XB = 8 * 258
            deps = xhc_users.pop(sl, ())
            if c == 0:
                ld0 = E(f"dP:xh{sl}", lambda e, sl=sl: e.dma_start(
                    out=XHC[:, sl * XB:sl * XB + 8 * 258].rearrange("p (j f) -> p j f", j=8)[:, :, 1:257],
                    in_=xh_d[:, :, 0:256].rearrange("j p f -> p j f")), deps=deps)
                ldz = E("V", lambda e, sl=sl: e.memset(
                    XHC[:, sl * XB:sl * XB + 8 * 258].rearrange("p (j f) -> p j f", j=8)[:, :, 0:1], 0.0),
                    deps=deps)
                ldxh = (ld0, ldz)
            else:
                ld0 = E(f"dP:xh{sl}", lambda e, c=c, sl=sl: e.dma_start(
                    out=XHC[:, sl * XB:sl * XB + 8 * 258].rearrange("p (j f) -> p j f", j=8)[:, :, 0:257],
                    in_=xh_d[:, :, c * 256 - 1:c * 256 + 256].rearrange("j p f -> p j f")), deps=deps)
                ldxh = (ld0,)
            xd_tok = []
            for jc in range(8):
                for st in range(2):
                    src0 = sl * XB + jc * 258 + 1 + st * 128
                    tt1 = E("T", lambda e, src0=src0: e.transpose(
                        pT[:, 0:128], XHC[:, src0:src0 + 128], identb[:]),
                        deps=ldxh + (c2,) + tuple(xd_tok[-1:]))
                    dst = xd1[:, st * 1024 + jc * 128: st * 1024 + (jc + 1) * 128]
                    w1E = bass.AP(w1tm[:].tensor, w1tm[:].offset + c * 2 * NH + st * NH + jc * 2,
                                  [list(w1tm[:].ap[0])] + [[1, 2], [0, 64]])
                    e1_ = E("V", lambda e, dst=dst, w1E=w1E: e.tensor_tensor(
                        out=dst.rearrange("p (a b) -> p a b", a=2), in0=bass.AP(
                            pT[:].tensor, pT[:].offset, [list(pT[:].ap[0])] + [[64, 2], [1, 64]]),
                        in1=w1E, op=ALU.mult), deps=(tt1, up_tok[-1]))
                    tt2 = E("T", lambda e, src0=src0: e.transpose(
                        pT[:, 128:256], XHC[:, src0 - 1:src0 + 127], identb[:]),
                        deps=ldxh + (c2, e1_))
                    dst2 = xd2[:, st * 1024 + jc * 128: st * 1024 + (jc + 1) * 128]
                    w2E = bass.AP(w2tm[:].tensor, w2tm[:].offset + c * 2 * NH + st * NH + jc * 2,
                                  [list(w2tm[:].ap[0])] + [[1, 2], [0, 64]])
                    e2_ = E("V", lambda e, dst2=dst2, w2E=w2E: e.tensor_tensor(
                        out=dst2.rearrange("p (a b) -> p a b", a=2), in0=bass.AP(
                            pT[:].tensor, pT[:].offset + 128, [list(pT[:].ap[0])] + [[64, 2], [1, 64]]),
                        in1=w2E, op=ALU.mult), deps=(tt2, up_tok[-1]))
                    xd_tok.extend((e1_, e2_))
            bt_tok = []
            for st in range(2):
                t1_ = E("T", lambda e, c=c, st=st: e.transpose(
                    pT[:, 256:320], btb[:, c * 256 + st * 128: c * 256 + st * 128 + 128],
                    identb[0:64, 0:64]), deps=tuple(rope_done) + (c2,) + tuple(bt_tok[-1:]))
                b1_ = E("V", lambda e, st=st: e.tensor_copy(out=btm[:, st * 64:(st + 1) * 64],
                                                            in_=pT[:, 256:320]), deps=(t1_,))
                t2_ = E("T", lambda e, c=c, st=st: e.transpose(
                    pT[:, 256:320], bshb[:, c * 256 + st * 128: c * 256 + st * 128 + 128],
                    identb[0:64, 0:64]), deps=(sh1, sh2, c2, b1_))
                b2_ = E("V", lambda e, st=st: e.tensor_copy(out=bshm[:, st * 64:(st + 1) * 64],
                                                            in_=pT[:, 256:320]), deps=(t2_,))
                bt_tok.extend((b1_, b2_))
            xw_tok = []
            for st in range(2):
                wdE = bass.AP(wdtm[:].tensor, wdtm[:].offset + c * 2 * NH + st * NH,
                              [list(wdtm[:].ap[0])] + [[1, NH], [0, 64]])
                x1_ = E("V", lambda e, st=st, wdE=wdE: e.tensor_tensor(
                    out=xw1[:, st * 1024:(st + 1) * 1024].rearrange("p (a b) -> p a b", a=NH),
                    in0=xd1[:, st * 1024:(st + 1) * 1024].rearrange("p (a b) -> p a b", a=NH),
                    in1=wdE, op=ALU.mult), deps=tuple(xd_tok) + (up_tok[-1],) + xhc_users.pop((sl, "w"), ()))
                x2_ = E("V", lambda e, st=st, wdE=wdE: e.tensor_tensor(
                    out=xw2[:, st * 1024:(st + 1) * 1024].rearrange("p (a b) -> p a b", a=NH),
                    in0=xd2[:, st * 1024:(st + 1) * 1024].rearrange("p (a b) -> p a b", a=NH),
                    in1=wdE, op=ALU.mult), deps=tuple(xd_tok) + (up_tok[-1],))
                xw_tok.extend((x1_, x2_))
            g_mm = []
            for gi, bsrc in ((0, btb), (1, bshb)):
                for st in range(2):
                    g_mm.append(E("T", lambda e, c=c, st=st, gi=gi, bsrc=bsrc: e.matmul(
                        pA2[:, (gi * 2 + st) * 256:(gi * 2 + st + 1) * 256],
                        bsrc[:, c * 256 + st * 128: c * 256 + st * 128 + 128],
                        ctb[:, c * 256:(c + 1) * 256], start=True, stop=True),
                        deps=tuple(rope_done) + ((sh1, sh2) if gi else ()) + tuple(bt_tok) + prev_evtoks))
            gt_tok = []
            for gi, gdst in ((0, g1b), (1, g2b)):
                for st in range(2):
                    gt_tok.append(E("V", lambda e, gi=gi, st=st, gdst=gdst: e.tensor_tensor(
                        out=gdst[:, st * 256:(st + 1) * 256],
                        in0=pA2[:, (gi * 2 + st) * 256:(gi * 2 + st + 1) * 256],
                        in1=triu[:, st * 256:(st + 1) * 256], op=ALU.mult),
                        deps=(g_mm[gi * 2 + st], c3) + prev_mask_last
                        + xhc_users.pop((sl, "g", gi, st), ())))
            cd1 = E("dS:cs", lambda e, c=c: e.dma_start(out=csf_d, in_=cs16[:, c * 256:(c + 1) * 256]),
                    deps=(t_cs,) + cs_dma_prev + ((tuple(rope_done) + (sh1, sh2)) if c == 0 else ()))
            cd2 = E("dS:cs", lambda e: e.dma_start(
                out=AR[0:1, 16384:16384 + TL][0:1, 0:TL] if False else AR[0:1, 16384:16384 + 4096],
                in_=csf_d.rearrange("h l -> (h l)").rearrange("(a q) -> a q", a=1)), deps=(cd1,))
            CSROW = AR[0:1, 16384:16384 + 4096]
            cs_dma_prev = (cd2,)
            ydiag_mm = {0: [], 1: []}
            yoff_mm = {}
            mask_toks = []

            def emit_bc1(i):
                return E("T", lambda e, i=i: e.matmul(
                    pF[:, (i % 2) * 256:(i % 2 + 1) * 256], onesf[:],
                    CSROW[0:1, i * 256:(i + 1) * 256], start=True, stop=True),
                    deps=(cd2, c0) + (tuple(d1_toks[-1:]) if d1_toks else ()))

            pend_bc1 = None
            for i in range(NH):
                par = (i % 2) * 512
                bc1 = pend_bc1 if pend_bc1 is not None else emit_bc1(i)
                for st in range(2):
                    dcol = cstm[:, c * 2 * NH + st * NH + i: c * 2 * NH + st * NH + i + 1]
                    d1 = E("V", lambda e, i=i, st=st, par=par, dcol=dcol: e.tensor_scalar(
                        out=dsc[:, par + st * 256:par + (st + 1) * 256],
                        in0=pF[:, (i % 2) * 256:(i % 2 + 1) * 256],
                        scalar1=dcol, scalar2=0.0, op0=ALU.subtract, op1=ALU.min),
                        deps=(bc1, up_tok[-1]) + tuple(last_d2.get((par, st), ())))
                    d1_toks.append(d1)
                    d2 = E("A", lambda e, st=st, par=par: e.activation(
                        out=fb[:, par + st * 256:par + (st + 1) * 256],
                        in_=dsc[:, par + st * 256:par + (st + 1) * 256],
                        func=AF.Exp), deps=(d1,) + last_d34.get((par, st), ()))
                    last_d2[(par, st)] = (d2,)
                    d3 = E("V" if st == 0 else "P", lambda e, st=st, par=par: e.tensor_tensor(
                        out=m1b[:, par + st * 256:par + (st + 1) * 256],
                        in0=fb[:, par + st * 256:par + (st + 1) * 256],
                        in1=g1b[:, st * 256:(st + 1) * 256], op=ALU.mult),
                        deps=(d2, gt_tok[0 * 2 + st]))
                    d4 = E("P", lambda e, st=st, par=par: e.tensor_tensor(
                        out=m2b[:, par + st * 256:par + (st + 1) * 256],
                        in0=fb[:, par + st * 256:par + (st + 1) * 256],
                        in1=g2b[:, st * 256:(st + 1) * 256], op=ALU.mult),
                        deps=(d2, gt_tok[1 * 2 + st]))
                    last_d34[(par, st)] = (d3, d4)
                    mask_toks.extend((d3, d4))
                pend_bc1 = emit_bc1(i + 1) if i + 1 < NH else None
                for lt in range(2):
                    pdst = pY if lt == 0 else pB2
                    first = (i == 0)
                    for sidx, (msrc, xsrc) in enumerate(((m1b, xd1), (m2b, xd2))):
                        for st in range(2):
                            last = (i == NH - 1 and sidx == 1 and st == 1)
                            ydiag_mm[lt].append(E("T", lambda e, i=i, lt=lt, st=st, msrc=msrc,
                                                  xsrc=xsrc, pdst=pdst, first=first, last=last,
                                                  sidx=sidx, par=par: e.matmul(
                                pdst[:, i * 64:(i + 1) * 64],
                                msrc[:, par + st * 256 + lt * 128: par + st * 256 + lt * 128 + 128],
                                xsrc[:, st * 1024 + i * 64: st * 1024 + i * 64 + 64],
                                start=(i % 8 == 0 and sidx == 0 and st == 0), stop=last,
                                skip_group_check=True),
                                deps=(mask_toks[-1], mask_toks[-2], mask_toks[-3], mask_toks[-4])
                                + tuple(xd_tok[-1:])
                                + ((xhc_users.pop((sl, "y", lt), ()) + H_prev) if i == 0 and sidx == 0 and st == 0 else ())))
            ev_toks = []
            for lt in range(2):
                yo = []
                for cg in range(2):
                    yo.append(E("T", lambda e, c=c, lt=lt, cg=cg: e.matmul(
                        pA2[:, cg * 512:(cg + 1) * 512],
                        ctb[:, c * 256 + lt * 128: c * 256 + lt * 128 + 128],
                        Hb[:, cg * 512:(cg + 1) * 512], start=True, stop=True),
                        deps=tuple(gt_tok) + H_prev + tuple(ev_toks) + tuple(rope_done)))
                epE = bass.AP(eptm[:].tensor, eptm[:].offset + c * 2 * NH + lt * NH,
                              [list(eptm[:].ap[0])] + [[1, NH], [0, 64]])
                s1_ = E("V", lambda e, epE=epE: e.tensor_tensor(
                    out=ysc[:].rearrange("p (a b) -> p a b", a=NH), in0=bass.AP(
                        pA2[:].tensor, pA2[:].offset, [list(pA2[:].ap[0])] + [[64, NH], [1, 64]]),
                    in1=epE, op=ALU.mult), deps=tuple(yo) + (up_tok[-1],) + tuple(ev_toks))
                s2_ = E("V", lambda e, lt=lt: e.tensor_tensor(
                    out=ytile[:, lt * 1024:(lt + 1) * 1024], in0=ysc[:],
                    in1=(pY if lt == 0 else pB2)[:, 0:1024], op=ALU.add),
                    deps=(s1_, ydiag_mm[lt][-1]))
                std = E(f"dS:y{lt}", lambda e, c=c, lt=lt: e.dma_start(
                    out=y_o[2 * c + lt], in_=ytile[:, lt * 1024:(lt + 1) * 1024]), deps=(s2_,))
                ev_toks.extend((s1_, s2_, std))
                ydmas.append(std)
            s_mm = []
            for sidx, (bsrc, xsrc) in enumerate(((btm, xw1), (bshm, xw2))):
                for st in range(2):
                    for cg in range(2):
                        s_mm.append(E("T", lambda e, sidx=sidx, st=st, cg=cg, bsrc=bsrc, xsrc=xsrc: e.matmul(
                            pY[0:64, cg * 512:(cg + 1) * 512], bsrc[:, st * 64:(st + 1) * 64],
                            xsrc[:, st * 1024 + cg * 512: st * 1024 + cg * 512 + 512],
                            start=(sidx == 0 and st == 0),
                            stop=(sidx == 1 and st == 1), skip_group_check=True),
                            deps=tuple(bt_tok) + tuple(xw_tok) + (ev_toks[1], ev_toks[4])))
            dmm = []
            for cg in range(2):
                decE = bass.AP(decrow[:].tensor, decrow[:].offset + c * NH + cg * (NH // 2),
                               [list(decrow[:].ap[0])] + [[1, NH // 2], [0, 64]])
                dmm.append(E("T", lambda e, decE=decE, cg=cg: e.matmul(
                    pB2[0:64, cg * 512:(cg + 1) * 512], onesf[0:1, 0:64], decE,
                    start=True, stop=True),
                    deps=tuple(dec_tok) + (ev_toks[4], ev_toks[5], c0)))
            dbc = dmm[-1]
            h1_ = E("V", lambda e: e.tensor_tensor(out=Hsc[:], in0=Hst[:], in1=pB2[0:64, 0:1024],
                                                   op=ALU.mult), deps=(dmm[0], dbc) + H_prev)
            h2_ = E("V", lambda e: e.tensor_tensor(out=Hst[:], in0=Hsc[:], in1=pY[0:64, 0:1024],
                                                   op=ALU.add), deps=(h1_, s_mm[-1]))
            h3_ = E("V", lambda e: e.tensor_copy(out=Hb[:], in_=Hst[:]), deps=(h2_,))
            H_prev = (h2_, h3_)
            xhc_users[sl] = (xd_tok[-1], xd_tok[-2], ydiag_mm[0][-1], ydiag_mm[1][-1])
            xhc_users[(sl, "w")] = (s_mm[-1],)
            xhc_users[(sl, "g", 0, 0)] = (ydiag_mm[0][-1], ydiag_mm[1][-1])
            xhc_users[(sl, "g", 0, 1)] = (ydiag_mm[0][-1], ydiag_mm[1][-1])
            xhc_users[(sl, "g", 1, 0)] = (ydiag_mm[0][-1], ydiag_mm[1][-1])
            xhc_users[(sl, "g", 1, 1)] = (ydiag_mm[0][-1], ydiag_mm[1][-1])
            xhc_users[(sl, "y", 0)] = (ev_toks[1],)
            xhc_users[(sl, "y", 1)] = (ev_toks[4],)
            prev_evtoks = tuple(ev_toks)
            prev_mask_last = tuple(mask_toks[-8:])
        p.finish(tuple(ydmas))
        p.build(es)
    return nc


def build_l1c():
    nc = bass.Bass("TRN2", target_bir_lowering=False)

    def din(name, shape, dt=F32):
        return nc.dram_tensor(name, shape, dt, kind="ExternalInput").ap()

    x = din("x", [DCH, 128, TC])
    y_d = din("y", [8, 128, 4096], BF16)
    sz_d = din("sz", [8, 128, 4096], BF16)
    woutg = din("woutg", [4, 32, 128, 512], F32)
    identb_d = din("identb", [128, 128])
    onesc_d = din("onesc", [128, 1])
    xo = nc.dram_tensor("xo", [DCH, 128, TC], F32, kind="ExternalOutput").ap()

    p = Prog(nc)
    es = ExitStack()
    with es:
        sb = lambda n, shape, dt: es.enter_context(nc.sbuf_tensor(n, shape, dt))
        ygfm = sb("ygfm", [128, 32 * TC], BF16)
        ystage = sb("ystage", [128, 2 * 4096], BF16)
        szst = sb("szst", [128, 2 * 4096], BF16)
        wst = sb("wst", [128, 2 * 32 * 512], BF16)
        xst = sb("xst", [128, TC], F32)
        rcol = sb("rcol", [128, 2], F32)
        epsc = sb("epsc", [128, 1], F32)
        ysq = sb("ysq", [128, 4096], BF16)
        identb = sb("s_identb", [128, 128], BF16)
        onesc = sb("s_onesc", [128, 1], F32)
        pA = es.enter_context(nc.psum_tensor("pA", [128, 512], F32))
        pB = es.enter_context(nc.psum_tensor("pB", [128, 512], F32))
        pC = es.enter_context(nc.psum_tensor("pC", [128, 512], F32))
        pD = es.enter_context(nc.psum_tensor("pD", [128, 512], F32))
        pT = es.enter_context(nc.psum_tensor("pT", [128, 2048], BF16))

        E = p.emit
        H = 512
        c0 = E("dP:c", lambda e: e.dma_start(out=identb[:], in_=identb_d))
        c1 = E("dA:c", lambda e: e.dma_start(out=onesc[:], in_=onesc_d))
        ce = E("V", lambda e: e.tensor_scalar(out=epsc[:], in0=onesc[:], scalar1=1e-6,
                                              scalar2=None, op0=ALU.mult), deps=(c1,))

        tr_done = []
        prev_use = {}
        for tt in range(8):
            sl = tt % 2
            deps = prev_use.pop(("y", sl), ())
            ldy = E(f"dS:y{sl}", lambda e, tt=tt, sl=sl: e.dma_start(
                out=ystage[:, sl * 4096:(sl + 1) * 4096], in_=y_d[tt]), deps=deps)
            ldz = E(f"dS:z{sl}", lambda e, tt=tt, sl=sl: e.dma_start(
                out=szst[:, sl * 4096:(sl + 1) * 4096], in_=sz_d[tt]),
                deps=prev_use.pop(("z", sl), ()))
            sq = E("A", lambda e, sl=sl: e.activation(
                out=ysq[:], in_=ystage[:, sl * 4096:(sl + 1) * 4096],
                func=AF.Square, accum_out=rcol[:, sl:sl + 1]), deps=(ldy,) + prev_use.pop("ysq", ()))
            r2 = E("A", lambda e, sl=sl: e.activation(out=rcol[:, sl:sl + 1], in_=rcol[:, sl:sl + 1],
                                                      func=AF.Ln, scale=1.0 / 4096, bias=epsc[:]),
                   deps=(sq, ce))
            r3 = E("A", lambda e, sl=sl: e.activation(out=rcol[:, sl:sl + 1], in_=rcol[:, sl:sl + 1],
                                                      func=AF.Exp, scale=-0.5), deps=(r2,))
            gy = E("V", lambda e, sl=sl: e.tensor_scalar(
                out=ystage[:, sl * 4096:(sl + 1) * 4096], in0=ystage[:, sl * 4096:(sl + 1) * 4096],
                scalar1=rcol[:, sl:sl + 1], scalar2=None, op0=ALU.mult), deps=(r3, sq))
            gz = E("V", lambda e, sl=sl: e.tensor_tensor(
                out=ystage[:, sl * 4096:(sl + 1) * 4096], in0=ystage[:, sl * 4096:(sl + 1) * 4096],
                in1=szst[:, sl * 4096:(sl + 1) * 4096], op=ALU.mult), deps=(gy, ldz))
            prev_use["ysq"] = (r2,)
            tts = []
            for jc in range(32):
                t1 = E("T", lambda e, tt=tt, sl=sl, jc=jc: e.transpose(
                    pT[:, (jc % 2) * 1024:(jc % 2) * 1024 + 128],
                    ystage[:, sl * 4096 + jc * 128: sl * 4096 + (jc + 1) * 128], identb[:]),
                    deps=(gz, c0) + tuple(tts[-2:-1]))
                t2 = E("V", lambda e, tt=tt, jc=jc: e.tensor_copy(
                    out=ygfm[:, jc * TC + tt * 128: jc * TC + (tt + 1) * 128],
                    in_=pT[:, (jc % 2) * 1024:(jc % 2) * 1024 + 128]), deps=(t1,))
                tts.append(t2)
            tr_done.extend(tts)
            prev_use[("y", sl)] = (gz,) + tuple(tts)
            prev_use[("z", sl)] = (gz,)
        out_dmas = []
        xst_users = {}
        wlast = {}
        TRD = tuple(tr_done)
        for grp in range(4):
            sl = grp % 2
            ldw = E(f"dP:w{sl}", lambda e, grp=grp, sl=sl: e.dma_start(
                out=wst[:, sl * 32 * 512:(sl + 1) * 32 * 512].rearrange("p (c f) -> p c f", c=32),
                in_=woutg[grp].rearrange("c p f -> p c f")), deps=wlast.pop(sl, ()))
            for mt4 in range(4):
                mt = grp * 4 + mt4
                pD_ = (pA, pB, pC, pD)[mt4]
                mm = []
                for h in range(2):
                    for kc in range(32):
                        mm.append(E("T", lambda e, kc=kc, sl=sl, h=h, pD_=pD_, mt4=mt4: e.matmul(
                            pD_[:], wst[:, sl * 32 * 512 + kc * 512 + mt4 * 128: sl * 32 * 512 + kc * 512 + (mt4 + 1) * 128],
                            ygfm[:, kc * TC + h * H: kc * TC + (h + 1) * H],
                            start=(kc == 0), stop=(kc == 31)),
                            deps=(ldw,) + TRD + (xst_users.get(("pd", mt4), ()) if kc == 0 else ())))
                    b = (2 * mt + h) % 2
                    dx = xst_users.pop(("x", b), ())
                    ldx = E(f"dS:x{b}", lambda e, mt=mt, h=h, b=b: e.dma_start(
                        out=xst[:, b * H:(b + 1) * H], in_=x[mt, :, h * H:(h + 1) * H]), deps=dx)
                    ev = E("V", lambda e, h=h, b=b, pD_=pD_: e.scalar_tensor_tensor(
                        out=xst[:, b * H:(b + 1) * H], in0=pD_[:], scalar=1.0,
                        in1=xst[:, b * H:(b + 1) * H], op0=ALU.mult, op1=ALU.add),
                        deps=(ldx, mm[-1]))
                    st = E(f"dS:o{b}", lambda e, mt=mt, h=h, b=b: e.dma_start(
                        out=xo[mt, :, h * H:(h + 1) * H], in_=xst[:, b * H:(b + 1) * H]), deps=(ev,))
                    out_dmas.append(st)
                    xst_users[("x", b)] = (st, ev)
                    xst_users[(mt4, h)] = (ev,)
                    xst_users[("pd", mt4)] = (ev,)
            wlast[sl] = tuple(xst_users[(m4, hh)][0] for m4 in range(4) for hh in range(2)
                              if (m4, hh) in xst_users)
        p.finish(out_dmas)
        p.build(es)
    return nc



_PROGS = {}


def _prog(name):
    if name not in _PROGS:
        _PROGS[name] = {
            "ffn0": lambda: build_ffn(F=7168, R=32, sel_val=1.0, out_scale=0.5)[0],
            "ffn1": lambda: build_ffn(F=5632, R=16, sel_val=2.0, out_scale=0.5)[0],
            "l0a": build_l0a, "l0b": build_l0b,
            "l1a": build_l1a, "l1b": build_l1b, "l1c": build_l1c,
        }[name]()
    return _PROGS[name]


def kernel(**inputs):
    ins = inputs
    CORES = list(range(NCORE))

    onesc = np.ones((128, 1), np.float32)
    onesf = np.ones((1, 128), np.float32)
    identf = np.eye(128, dtype=np.float32)

    wd0 = prep_ffn_weights(ins["L0_ffn_Wg"], ins["L0_ffn_W1"], ins["L0_ffn_W2"], ins["L0_ffn_W3"],
                           ins["L0_ffn_A1"], ins["L0_ffn_B1"], ins["L0_ffn_A2"], ins["L0_ffn_B2"],
                           ins["L0_ffn_A3"], ins["L0_ffn_B3"], R=32, sel_val=1.0)
    wd1 = prep_ffn_weights(ins["L1_ffn_gate_w"], ins["L1_ffn_w1"], ins["L1_ffn_w2"], ins["L1_ffn_w3"],
                           ins["L1_ffn_a1"], ins["L1_ffn_b1"], ins["L1_ffn_a2"], ins["L1_ffn_b2"],
                           ins["L1_ffn_a3"], ins["L1_ffn_b3"], R=16, sel_val=2.0)

    def run(name, maps):
        nc = _prog(name)
        return run_bass_kernel_spmd(nc, maps, core_ids=CORES).results

    xs = x_to_cores(ins["L0_attn_u"])
    r = run("ffn0", [dict(wd0, x=xs[c]) for c in CORES])
    xs = [np.asarray(r[c]["xo"]) for c in CORES]

    iota256 = np.broadcast_to(np.arange(256, dtype=np.float32), (32, 256)).copy()
    iotap = np.broadcast_to(np.arange(1, TC + 1, dtype=np.float32), (32, TC)).copy()
    rstm = np.ones((64, TC), np.float32)
    rstm[:, 0::256] = 0.0
    com0 = dict(
        btre=_c(ins["L0_attn_B_re"].T.reshape(DCH, 128, 32)),
        btim=_c(ins["L0_attn_B_im"].T.reshape(DCH, 128, 32)),
        lamre=ins["L0_attn_Lambda_unconstrained_re"].reshape(32, 1),
        lamim=ins["L0_attn_Lambda_im"].reshape(32, 1),
        logdel=ins["L0_attn_log_Delta"][:32].reshape(32, 1),
        iota256=iota256, rstm=rstm, onesc=onesc, onesf=onesf,
    )
    r = run("l0a", [dict(com0, x=xs[c]) for c in CORES])
    xloc = [np.asarray(r[c]["xloc"]) for c in CORES]
    u2 = [np.asarray(r[c]["u_out"]) for c in CORES]

    ct = _c(np.concatenate([ins["L0_attn_C_re"].T, ins["L0_attn_C_im"].T], 0)).reshape(64, DCH * 128)
    dcol = _c(ins["L0_attn_D"].reshape(DCH, 128).T)
    maps = []
    for c in CORES:
        b, q = c // 4, c % 4
        sprev = np.zeros((64, 4), np.float32)
        for j in range(1, 4):
            if q - j >= 0:
                sprev[:, j - 1] = xloc[c - j][:, -1]
        maps.append(dict(x=xs[c], u_in=u2[c], xloc=xloc[c], sprev=sprev, ct=ct, dcol=dcol,
                         lamre=com0["lamre"], lamim=com0["lamim"], logdel=com0["logdel"],
                         iotap=iotap, onesc=onesc))
    r = run("l0b", maps)
    xs = [np.asarray(r[c]["xo"]) for c in CORES]

    r = run("ffn0", [dict(wd0, x=xs[c]) for c in CORES])
    xs = [np.asarray(r[c]["xo"]) for c in CORES]

    r = run("ffn1", [dict(wd1, x=xs[c]) for c in CORES])
    xs = [np.asarray(r[c]["xo"]) for c in CORES]

    W = ins["L1_attn_W_in"]
    com1 = dict(
        winzB=_c(W[:, 0:4096].reshape(DCH, 128, 8, 512).transpose(2, 0, 1, 3)),
        winxB=_c(W[:, 4096:8192].reshape(DCH, 128, 32, 128).transpose(2, 0, 1, 3)),
        wins=_c(W[:, 8192:8480].reshape(DCH, 128, 288)),
        alog=ins["L1_attn_A_log"].reshape(64, 1),
        dtb=ins["L1_attn_dt_bias"].reshape(64, 1),
        bnb=ins["L1_attn_B_norm_bias"].reshape(64, 1),
        cnb=ins["L1_attn_C_norm_bias"].reshape(64, 1),
        onesc=onesc, onesf=onesf,
    )
    r = run("l1a", [dict(com1, x=xs[c]) for c in CORES])
    xh_p = [np.asarray(r[c]["o_xh"]) for c in CORES]
    sz_p = [np.asarray(r[c]["o_sz"]) for c in CORES]
    sm_p = [np.asarray(r[c]["o_sm"]) for c in CORES]

    trj = np.zeros((2, 128, 256), np.float32)
    for st in range(2):
        for rr in range(128):
            trj[st, rr, st * 128 + rr:] = 1.0
    rst = np.ones((NH, TL), np.float32)
    rst[:, 0::256] = 0.0
    xh_full = {}
    smf = {}
    for b in range(2):
        xh_full[b] = np.concatenate([xh_p[b * 4 + q] for q in range(4)], axis=2)
        smf[b] = np.concatenate([sm_p[b * 4 + q] for q in range(4)], axis=2)
    maps = []
    for c in CORES:
        b, hg = c // 4, c % 4
        maps.append(dict(
            xh=_c(xh_full[b][hg * 8:(hg + 1) * 8]),
            bn=_c(smf[b][0, 0:64]), cn=_c(smf[b][1, 0:64]),
            thw=_c(smf[b][6, 0:32]),
            da16=_c(smf[b][3, hg * 16:(hg + 1) * 16]),
            w1t16=_c(smf[b][4, hg * 16:(hg + 1) * 16]),
            w2t16=_c(smf[b][5, hg * 16:(hg + 1) * 16]),
            triu=trj, rst256=rst, identf=identf, identb=identf, onesf=onesf,
        ))
    r = run("l1b", maps)
    y_p = [np.asarray(r[c]["y"]) for c in CORES]

    wog = _c(ins["L1_attn_W_out"].reshape(32, 128, 4, 512).transpose(2, 0, 1, 3))
    yfull = {}
    for b in range(2):
        yfull[b] = np.concatenate([y_p[b * 4 + hg] for hg in range(4)], axis=2)
    maps = []
    for c in CORES:
        b, q = c // 4, c % 4
        maps.append(dict(x=xs[c], y=_c(yfull[b][q * 8:(q + 1) * 8]), sz=sz_p[c],
                         woutg=wog, identb=identf, onesc=onesc))
    r = run("l1c", maps)
    xs = [np.asarray(r[c]["xo"]) for c in CORES]

    r = run("ffn1", [dict(wd1, x=xs[c]) for c in CORES])
    xs = [np.asarray(r[c]["xo"]) for c in CORES]

    return x_from_cores(xs)
